# version 09
# baseline (speedup 1.0000x reference)
import contextlib
import numpy as np
import concourse.bass as bass
import concourse.mybir as mybir
from concourse.bass_utils import run_bass_kernel_spmd

F32 = mybir.dt.float32
F16 = mybir.dt.float16
OP = mybir.AluOpType
AF = mybir.ActivationFunctionType
AX = mybir.AxisListType

NCORES = 8
TC = 1024
C = 2048
KC = 16
NH = 64
CH = 128
NCH = TC // CH
TT = 256
NTT = TC // TT
MF = 5632
FT = 256
NFT = TC // FT
LN_EPS = 1e-5
GN_EPS = 1e-5 * 64.0
FFN_EPS = 1e-8

ENGINES = ("sync", "tensor", "vector", "scalar", "gpsimd")

_BUILD_OPTS = {"taps": (), "stop_after": None, "p3_skip_proj": False,
               "p3_skip_scan": False, "p3_ntt": None, "scan_lvl": 4,
               "scan_parts": "dABTe"}


class Prog:
    def __init__(self):
        self.q = {e: [] for e in ENGINES}
        self.cnt = {}
        self.semnames = []

    def sem(self, name):
        self.semnames.append(name)
        self.cnt[name] = 0

    def do(self, eng, fn, inc=None):
        nm = "p_" + eng
        incs = list(inc or [])
        if not incs:
            incs.append((nm, 1))
        if any(n == nm for n, _ in incs):
            self.wait(eng, nm, self.cnt[nm])
        for nm2, k in incs:
            self.cnt[nm2] += k
        toks = {nm2: self.cnt[nm2] for nm2, _ in incs}

        def run(e, S, fn=fn, incs=incs):
            r = fn(e, S)
            for nm2, k in incs:
                r = r.then_inc(S[nm2], k)
        self.q[eng].append(run)
        return toks

    def dob(self, eng, fn):
        nm = "p_" + eng
        return self.do(eng, fn, inc=[(nm, 1)])[nm]

    def wait(self, eng, nm, val):
        if val <= 0:
            return

        def run(e, S, nm=nm, val=val):
            e.wait_ge(S[nm], val)
        self.q[eng].append(run)

    def tok(self, nm):
        return self.cnt[nm]

    def wait_all(self, eng, others):
        for o in others:
            nm = o if (o.startswith("d") or o == "CC") else "p_" + o
            self.wait(eng, nm, self.cnt[nm])

    def mark(self, eng):
        nm = "p_" + eng
        self.cnt[nm] += 1
        val = self.cnt[nm]

        def run(e, S, nm=nm):
            e.drain()
            e.sem_inc(S[nm], 1)
        self.q[eng].append(run)
        return val

    def softmark(self, eng):
        nm = "p_" + eng
        self.cnt[nm] += 1
        val = self.cnt[nm]

        def run(e, S, nm=nm):
            e.sem_inc(S[nm], 1)
        self.q[eng].append(run)
        return val

    def barrier(self, engines=ENGINES):
        toks = {eng: self.mark(eng) for eng in engines}
        for eng in engines:
            for other in engines:
                if other != eng:
                    self.wait(eng, "p_" + other, toks[other])


def host_inputs(inputs):
    f = lambda k: np.ascontiguousarray(np.asarray(inputs[k], dtype=np.float32))
    x = f("L0_attn_x")[0]
    wr, wk, wv, wg, wo = (f("L0_attn_w" + n) for n in ("r", "k", "v", "g", "o"))
    w1 = f("L0_attn_w1")
    w2 = np.ascontiguousarray(f("L0_attn_w2").reshape(160, 2048))
    tdw1, tdw2 = f("L0_attn_td_w1"), f("L0_attn_td_w2")
    fw1, fw2, fw3 = f("L0_ffn_w1"), f("L0_ffn_w2"), f("L0_ffn_w3")

    def npanels(w):
        return np.ascontiguousarray(
            w.reshape(KC, 128, KC, 128).transpose(2, 1, 0, 3).reshape(KC, C, 128))

    def ppack(v):
        return np.ascontiguousarray(v.reshape(KC, 128).T)

    vec_cols = [ppack(f("L0_attn_time_maa_" + n)[0, 0]) for n in ("x", "w", "k", "v", "r", "g")]
    vec_cols.append(ppack(f("L0_attn_time_decay")[0, 0]))
    vec_cols.append(ppack(f("L0_attn_time_faaaa").reshape(C)))
    vecs = np.concatenate(vec_cols, axis=1)

    lnw_bc = np.ascontiguousarray(np.broadcast_to(f("L0_attn_ln_w"), (128, C)))
    lnb_bc = np.ascontiguousarray(np.broadcast_to(f("L0_attn_ln_b"), (128, C)))
    ident = np.eye(128, dtype=np.float32)
    ones128 = np.ones((128, 128), dtype=np.float32)
    pf, ff = np.arange(128)[:, None], np.arange(128)[None, :]
    mask4 = ((pf % 64) < (ff % 64)).astype(np.float32)
    dcols2 = np.zeros((128, 2), np.float32)
    dcols2[:64, 0] = 1.0
    dcols2[64:, 1] = 1.0

    epss = np.zeros((128, 2), np.float32)
    epss[:, 0] = LN_EPS
    epss[:, 1] = GN_EPS
    base = {
        "epss": epss,
        "wrp": npanels(wr), "wkp": npanels(wk), "wvp": npanels(wv), "wgp": npanels(wg),
        "wo": wo, "w1": w1, "w2f": w2, "tdw1": tdw1, "tdw2": tdw2,
        "vecs": vecs, "lnw_bc": lnw_bc, "lnb_bc": lnb_bc,
        "ident": ident, "ones128": ones128, "mask4": mask4, "dcols2": dcols2,
        "fw1": fw1, "fw2": fw2, "fw3": fw3,
    }
    in_maps = []
    for c in range(NCORES):
        xin = np.zeros((1032, C), np.float32)
        t0 = c * TC
        xin[1:1 + TC] = x[t0:t0 + TC]
        if c > 0:
            xin[0] = x[t0 - 1]
        lt = np.zeros((128, 8), np.float32)
        lt[:, :c] = 1.0
        m = dict(base)
        m["xin"] = xin
        m["lt"] = lt
        m["omlt"] = 1.0 - lt
        in_maps.append(m)
    return in_maps


def build_program(taps=(), stop_after=None, p3_skip_proj=False, p3_skip_scan=False, p3_ntt=None, scan_lvl=4,
                  scan_parts="dABTe"):
    nc = bass.Bass("TRN2", target_bir_lowering=False)
    p = Prog()

    D = {}
    def din(name, shape, dt=F32):
        D[name] = nc.dram_tensor(name, shape, dt, kind="ExternalInput").ap()
    def dout(name, shape, dt=F32):
        D[name] = nc.dram_tensor(name, shape, dt, kind="ExternalOutput").ap()
    def dint(name, shape, dt=F32):
        D[name] = nc.dram_tensor(name, shape, dt).ap()

    din("xin", [1032, C])
    for n in ("wrp", "wkp", "wvp", "wgp"):
        din(n, [KC, C, 128])
    din("wo", [C, C])
    din("w1", [C, 160]); din("w2f", [160, C])
    din("tdw1", [C, 64]); din("tdw2", [64, C])
    din("vecs", [128, 128]); din("lnw_bc", [128, C]); din("lnb_bc", [128, C])
    din("ident", [128, 128]); din("ones128", [128, 128]); din("mask4", [128, 128])
    din("dcols2", [128, 2]); din("lt", [128, 8]); din("omlt", [128, 8]); din("epss", [128, 2])
    din("fw1", [MF, C]); din("fw2", [MF, C]); din("fw3", [C, MF])
    dout("outp", [TC, C])
    dint("sc_xn", [KC, 128, 1032])
    dint("sc_raw", [9, 128, C])
    dint("sc_y0", [NCH, 128, C])
    dint("sc_rc", [NCH, 128, C])
    dint("sc_g", [NCH, 128, C])
    dint("sc_s", [NCH, 128, C])
    dint("sc_qw1", [MF, C], F16)
    dint("sc_qw2", [MF, C], F16)
    dint("sc_qw3", [C, MF], F16)
    dint("cc_in", [2176, 64])
    dint("cc_out", [NCORES * 2176, 64])
    dint("dumA", [4096, C])
    dint("dumB", [4096, C])

    es_all = contextlib.ExitStack()
    PS = [es_all.enter_context(nc.psum_tensor(f"pb{i}", [128, 512], F32)) for i in range(8)]

    SB = {}
    def sb(es, name, shape, dt=F32):
        SB[name] = es.enter_context(nc.sbuf_tensor(name, shape, dt))
        return SB[name]

    sb(es_all, "idents", [128, 128])
    sb(es_all, "ones128s", [128, 128])
    sb(es_all, "mask4s", [128, 128])
    sb(es_all, "vecss", [128, 128])
    sb(es_all, "dcols2s", [128, 2]); sb(es_all, "epsss", [128, 2])
    sb(es_all, "lts", [128, 8]); sb(es_all, "omlts", [128, 8])
    sb(es_all, "g_st", [128, 16])
    sb(es_all, "carry", [128, KC]); sb(es_all, "pacc", [128, KC])
    sb(es_all, "negs", [128, 3]); sb(es_all, "emid", [128, 1]); sb(es_all, "pch", [128, 1])
    sb(es_all, "Sloc", [128, KC * 64])

    for e in ENGINES:
        p.sem("p_" + e)
    p.sem("dS"); p.sem("dG"); p.sem("dW"); p.sem("dX"); p.sem("CC")
    for s in ("lx0", "lx1", "rw0", "rw1", "pn0", "pn1", "r0", "r1", "r2", "r3"):
        p.sem(s)

    VOFF = {"x": 0, "w": 16, "k": 32, "v": 48, "r": 64, "g": 80, "tdec": 96, "u": 112}

    def vcol(n, kc):
        return SB["vecss"][:, VOFF[n] + kc:VOFF[n] + kc + 1]

    def dma(eng, sem, out_fn, in_fn, transpose=False):
        def fn(e, S):
            return e.dma_start(out=out_fn(), in_=in_fn(), transpose=transpose)
        return p.do(eng, fn, inc=[(sem, 16)])[sem]

    for dst, src in [("vecss", "vecs"), ("idents", "ident"), ("ones128s", "ones128"),
                     ("mask4s", "mask4"), ("dcols2s", "dcols2"), ("lts", "lt"), ("omlts", "omlt"), ("epsss", "epss")]:
        dma("sync", "dS", lambda dst=dst: SB[dst][:], lambda src=src: D[src][:])
    t_const = p.tok("dS")
    p.wait("vector", "dS", t_const)
    p.do("vector", lambda v, S: v.memset(SB["g_st"][:], 0.0))
    wv = [0]

    def wslot(n=32):
        c = wv[0]
        wv[0] += n
        return c

    for _ in range(2):
        p.do("vector", lambda v, S, c=wslot(): v.reduce_sum(SB["Sloc"][:, c:c + 1], SB["idents"][:], axis=AX.X))
        p.do("vector", lambda v, S, c=wslot(): v.tensor_tensor(SB["Sloc"][:, c:c + 32], SB["idents"][:, 0:32], SB["idents"][:, 0:32], OP.mult))
        p.do("vector", lambda v, S, c=wslot(): v.tensor_scalar(SB["Sloc"][:, c:c + 32], SB["idents"][:, 0:32], 2.0, None, OP.mult))
        p.do("vector", lambda v, S, c=wslot(): v.tensor_copy(SB["Sloc"][:, c:c + 32], SB["idents"][:, 0:32]))
        p.do("vector", lambda v, S, c=wslot(): v.reciprocal(SB["Sloc"][:, c:c + 1], SB["epsss"][:, 0:1]))
        p.do("vector", lambda v, S, c=wslot(): v.tensor_tensor_scan(SB["Sloc"][:, c:c + 32], SB["idents"][:, 0:32], SB["idents"][:, 0:32], 0.0, OP.add, OP.bypass))
        p.do("vector", lambda v, S, c=wslot(): v.scalar_tensor_tensor(SB["Sloc"][:, c:c + 32], SB["idents"][:, 0:32], 1.0, SB["idents"][:, 0:32], OP.mult, OP.add))
    t_wv = p.mark("vector")
    p.wait("scalar", "dS", t_const)
    for fn_ in (AF.Tanh, AF.Exp, AF.Sigmoid, AF.Silu, AF.Sqrt, AF.Square, AF.Abs, AF.Identity):
        p.do("scalar", lambda s, S, fn_=fn_, c=wslot(): s.activation(
            SB["Sloc"][:, c:c + 32], SB["mask4s"][:, 0:32], fn_))
    t_wa = p.mark("scalar")
    p.wait("tensor", "dS", t_const)
    for _ in range(4):
        p.do("tensor", lambda t, S: t.matmul(PS[0][:, 0:128], SB["idents"][:], SB["idents"][:], start=True, stop=True))
        p.do("tensor", lambda t, S: t.transpose(PS[1][:, 0:128], SB["idents"][:], SB["idents"][:]))
    t_wt = p.mark("tensor")
    p.wait("gpsimd", "dS", t_const)
    for wg in range(2):
        p.do("gpsimd", lambda g, S, c=wslot(): g.tensor_copy(
            SB["Sloc"][:, c:c + 32], SB["idents"][:, 0:32]))
        p.do("gpsimd", lambda g, S, c=wslot(): g.tensor_scalar(
            SB["Sloc"][:, c:c + 32], SB["idents"][:, 0:32], 1.0, None, OP.mult))
    t_wg = p.mark("gpsimd")
    for eng in ENGINES:
        p.wait(eng, "p_vector", t_wv)
        p.wait(eng, "p_scalar", t_wa)
        p.wait(eng, "p_tensor", t_wt)
        p.wait(eng, "p_gpsimd", t_wg)

    with contextlib.ExitStack() as es1:
        sb(es1, "xt0", [128, C]); sb(es1, "xt1", [128, C])
        sb(es1, "xnt", [128, C]); sb(es1, "stage", [128, C])
        sb(es1, "trash1", [128, C]); sb(es1, "statw", [128, 9 * 16])
        for it_ in range(18):
            ppass, i = divmod(it_, 9)
            slot = it_ % 2
            xt = "xt0" if slot == 0 else "xt1"
            lx, rw = "lx%d" % slot, "rw%d" % slot
            w = 128 if i < 8 else 8
            p.wait_all("sync", ["vector", "scalar"])
            p.wait("sync", rw, p.tok(rw))
            tl = dma("sync", lx, lambda xt=xt, w=w: SB[xt][0:w, :],
                     lambda i=i, w=w: D["xin"][i * 128:i * 128 + w, :])
            p.wait("gpsimd", lx, tl)
            dma("gpsimd", rw, lambda i=i, w=w: D["sc_raw"][i, 0:w, :],
                lambda xt=xt, w=w: SB[xt][0:w, :])
            p.wait("vector", lx, tl)
            p.do("vector", lambda v, S, xt=xt, w=w: v.reduce_sum(SB["g_st"][0:w, 0:1], SB[xt][0:w, :], axis=AX.X))
            p.do("vector", lambda v, S, xt=xt, w=w: v.tensor_tensor(
                SB["trash1"][0:w, :], SB[xt][0:w, :], SB[xt][0:w, :], OP.mult))
            p.do("vector", lambda v, S, w=w: v.reduce_sum(SB["g_st"][0:w, 1:2], SB["trash1"][0:w, :], axis=AX.X))
            p.do("vector", lambda v, S, w=w: v.tensor_scalar(SB["g_st"][0:w, 2:3], SB["g_st"][0:w, 0:1], 1.0 / C, None, OP.mult))
            p.do("vector", lambda v, S, w=w: v.tensor_tensor(SB["g_st"][0:w, 3:4], SB["g_st"][0:w, 2:3], SB["g_st"][0:w, 2:3], OP.mult))
            tV = p.dob("vector", lambda v, S, w=w: v.tensor_scalar(
                SB["g_st"][0:w, 4:5], SB["g_st"][0:w, 1:2], 1.0 / C, SB["g_st"][0:w, 3:4], OP.mult, OP.subtract))
            p.wait("scalar", "p_vector", tV)
            tA = p.dob("scalar", lambda s, S, w=w: s.activation(
                SB["g_st"][0:w, 8:9], SB["g_st"][0:w, 4:5], AF.Sqrt, bias=SB["epsss"][0:w, 0:1], scale=1.0))
            p.wait("vector", "p_scalar", tA)
            p.do("vector", lambda v, S, w=w: v.reciprocal(SB["g_st"][0:w, 12:13], SB["g_st"][0:w, 8:9]))
            p.do("vector", lambda v, S, i=i: v.tensor_copy(SB["statw"][:, i * 16:(i + 1) * 16], SB["g_st"][:, 0:16]))
            tV = p.dob("vector", lambda v, S, xt=xt, w=w: v.tensor_scalar(
                SB["xnt"][0:w, :], SB[xt][0:w, :], SB["g_st"][0:w, 2:3], SB["g_st"][0:w, 12:13], OP.subtract, OP.mult))
            p.wait("tensor", "p_vector", tV)
            p.wait("vector", "dG", p.tok("dG"))
            for kc in range(KC):
                pb = PS[kc % 2]
                p.wait_all("tensor", ["vector"])
                tT = p.dob("tensor", lambda t, S, kc=kc, pb=pb, w=w: t.transpose(
                    pb[:, 0:w], SB["xnt"][0:w, kc * 128:(kc + 1) * 128],
                    SB["idents"][0:w, 0:w]))
                p.wait("vector", "p_tensor", tT)
                tV = p.dob("vector", lambda v, S, kc=kc, pb=pb, w=w: v.tensor_copy(
                    SB["stage"][:, kc * 128:kc * 128 + w], pb[:, 0:w]))
            p.wait("gpsimd", "p_vector", tV)
            for kc in range(KC):
                dma("gpsimd", "dG", lambda kc=kc, i=i, w=w: D["sc_xn"][kc, :, i * 128:i * 128 + w],
                    lambda kc=kc, w=w: SB["stage"][:, kc * 128:kc * 128 + w])

    p.wait("gpsimd", "dG", p.tok("dG"))
    p.wait("gpsimd", "rw0", p.tok("rw0"))
    p.wait("gpsimd", "rw1", p.tok("rw1"))
    p.barrier()
    if stop_after == "P1":
        dma("gpsimd", "dG", lambda: D["sc_y0"][0, :, 0:144], lambda: SB["statw"][:])
        p.barrier()
        _emit_taps(nc, p, D, SB, taps, dma)
        _finalize(nc, p)
        es_all.close()
        return nc

    p.do("vector", lambda v, S: v.memset(SB["Sloc"][:], 0.0))
    p.do("vector", lambda v, S: v.memset(SB["carry"][:], 0.0))
    p.do("vector", lambda v, S: v.memset(SB["pacc"][:], 1.0))
    es34 = contextlib.ExitStack()
    sb(es34, "w1s", [128, KC * 160])
    sb(es34, "w2s", [128, C])
    sb(es34, "w2s4", [32, C])
    sb(es34, "tdw1s", [128, KC * 64])
    sb(es34, "tdw2s", [64, C])
    sb(es34, "th_s", [64, TT])
    sb(es34, "m_s", [128, TT]); sb(es34, "m4_s", [32, TT])
    sb(es34, "cumt", [128, KC * 257])
    sb(es34, "xn_tile", [128, KC * 257])
    sb(es34, "xinp_a", [128, KC * TT]); sb(es34, "xinp_b", [128, KC * TT])
    sb(es34, "panel_a", [128, KC * 128]); sb(es34, "panel_b", [128, KC * 128])
    sb(es34, "xxx_a", [128, TT]); sb(es34, "xxx_b", [128, TT])
    sb(es34, "mixe", [128, TT])
    sb(es34, "rate_fm", [128, TT])
    sb(es34, "r_fm", [128, KC * TT]); sb(es34, "k_fm", [128, KC * TT])
    sb(es34, "vg_fm", [128, KC * TT])
    sb(es34, "v_tm0", [128, C]); sb(es34, "v_tm1", [128, C])
    sb(es34, "g_tm", [128, 512])
    sb(es34, "y0t", [128, 512]); sb(es34, "rc_t", [128, 512])
    sb(es34, "Er", [128, CH]); sb(es34, "Ek", [128, CH]); sb(es34, "Eh", [128, CH])
    sb(es34, "rt_t", [128, CH]); sb(es34, "kt_t", [128, CH]); sb(es34, "kh_t", [128, CH])
    sb(es34, "rmid", [128, 64]); sb(es34, "ruk", [128, CH])
    sb(es34, "A2m", [128, CH]); sb(es34, "Bm", [64, CH]); sb(es34, "khTm", [128, CH])
    sb(es34, "d_m", [128, 32]); sb(es34, "dv", [128, CH])

    for kc in range(KC):
        dma("sync", "dW", lambda kc=kc: SB["w1s"][:, kc * 160:(kc + 1) * 160],
            lambda kc=kc: D["w1"][kc * 128:(kc + 1) * 128, :])
        dma("sync", "dW", lambda kc=kc: SB["tdw1s"][:, kc * 64:(kc + 1) * 64],
            lambda kc=kc: D["tdw1"][kc * 128:(kc + 1) * 128, :])
    dma("sync", "dW", lambda: SB["w2s"][:], lambda: D["w2f"][0:128, :])
    dma("sync", "dW", lambda: SB["w2s4"][:], lambda: D["w2f"][128:160, :])
    dma("sync", "dW", lambda: SB["tdw2s"][:], lambda: D["tdw2"][:])
    t_c34 = p.tok("dW")
    for e in ("tensor", "vector", "scalar"):
        p.wait(e, "dW", t_c34)

    PROJ = [("w", 0, None), ("r", 3, "wrp"), ("k", 1, "wkp"), ("v", 2, "wvp"), ("g", 4, "wgp")]

    def emit_mix_and_xinput(fam, xinp_name, vname):
        for kc in range(KC):
            koff = kc * 257
            p.wait_all("tensor", ["vector"])
            if fam < 4:
                def mixmm(t, S, kc=kc, fam=fam):
                    tp = (96, 0) if fam == 3 else None
                    return t.matmul(PS[2][:, 0:TT],
                                    SB["w2s"][fam * 32:(fam + 1) * 32, kc * 128:(kc + 1) * 128],
                                    SB["m_s"][fam * 32:(fam + 1) * 32, :],
                                    start=True, stop=True, tile_position=tp)
            else:
                def mixmm(t, S, kc=kc):
                    return t.matmul(PS[2][:, 0:TT], SB["w2s4"][0:32, kc * 128:(kc + 1) * 128],
                                    SB["m4_s"][0:32, :], start=True, stop=True)
            tT = p.dob("tensor", mixmm)
            p.wait("scalar", "p_tensor", tT)
            tA = p.dob("scalar", lambda s, S, kc=kc, vname=vname: s.activation(
                SB["mixe"][:], PS[2][:, 0:TT], AF.Identity, bias=vcol(vname, kc), scale=1.0))
            p.wait("vector", "p_scalar", tA)
            xxx = "xxx_a" if kc % 2 == 0 else "xxx_b"
            p.do("vector", lambda v, S, koff=koff, xxx=xxx: v.tensor_tensor(
                SB[xxx][:], SB["xn_tile"][:, koff:koff + TT],
                SB["xn_tile"][:, koff + 1:koff + 1 + TT], OP.subtract))
            p.do("vector", lambda v, S, xxx=xxx: v.tensor_tensor(
                SB[xxx][:], SB[xxx][:], SB["mixe"][:], OP.mult))
            p.dob("vector", lambda v, S, kc=kc, koff=koff, xxx=xxx, xinp_name=xinp_name: v.tensor_tensor(
                SB[xinp_name][:, kc * TT:(kc + 1) * TT], SB[xxx][:],
                SB["xn_tile"][:, koff + 1:koff + 1 + TT], OP.add))

    def emit_proj_mms(wname, xinp_name, dest):
        for n in range(KC):
            pan = "panel_a" if n % 2 == 0 else "panel_b"
            pnm = "pn%d" % (n % 2)
            p.wait_all("sync", ["tensor"])
            tl = dma("sync", pnm, lambda pan=pan: SB[pan][:],
                     lambda wname=wname, n=n: D[wname][n, :, :])
            p.wait("tensor", pnm, tl)
            pb = PS[4 + n % 2]
            p.wait_all("tensor", ["vector"])
            for kc in range(KC):
                inc = [("p_tensor", 1)] if kc == KC - 1 else None
                toks = p.do("tensor", lambda t, S, pan=pan, kc=kc, pb=pb, xinp_name=xinp_name: t.matmul(
                    pb[:, 0:TT], SB[pan][:, kc * 128:(kc + 1) * 128],
                    SB[xinp_name][:, kc * TT:(kc + 1) * TT],
                    start=(kc == 0), stop=(kc == KC - 1)), inc=inc)
            p.wait("vector", "p_tensor", toks["p_tensor"])
            p.dob("vector", lambda v, S, n=n, pb=pb, dest=dest: v.tensor_copy(
                SB[dest][:, n * TT:(n + 1) * TT], pb[:, 0:TT]))

    def emit_rate(xinp_name):
        p.wait_all("tensor", ["vector", "scalar"])
        for kc in range(KC):
            inc = [("p_tensor", 1)] if kc == KC - 1 else None
            toks = p.do("tensor", lambda t, S, kc=kc, xinp_name=xinp_name: t.matmul(
                PS[6][0:64, 0:TT], SB["tdw1s"][:, kc * 64:(kc + 1) * 64],
                SB[xinp_name][:, kc * TT:(kc + 1) * TT],
                start=(kc == 0), stop=(kc == KC - 1)), inc=inc)
        p.wait("scalar", "p_tensor", toks["p_tensor"])
        tA = p.dob("scalar", lambda s, S: s.activation(SB["th_s"][:], PS[6][0:64, 0:TT], AF.Tanh))
        p.wait("tensor", "p_scalar", tA)
        for n in range(KC):
            pb = PS[4 + n % 2]
            p.wait_all("tensor", ["vector", "scalar"])
            tT = p.dob("tensor", lambda t, S, n=n, pb=pb: t.matmul(
                pb[:, 0:TT], SB["tdw2s"][0:64, n * 128:(n + 1) * 128], SB["th_s"][:],
                start=True, stop=True))
            p.wait("scalar", "p_tensor", tT)
            tA = p.dob("scalar", lambda s, S, n=n, pb=pb: s.activation(
                SB["rate_fm"][:], pb[:, 0:TT], AF.Exp, bias=vcol("tdec", n), scale=1.0))
            p.wait("vector", "p_scalar", tA)
            noff = n * 257
            p.do("vector", lambda v, S, n=n, noff=noff: v.tensor_copy(
                SB["cumt"][:, noff:noff + 1], SB["carry"][:, n:n + 1]))
            p.do("vector", lambda v, S, noff=noff: v.tensor_tensor_scan(
                SB["cumt"][:, noff + 1:noff + 1 + TT], SB["rate_fm"][:], SB["rate_fm"][:],
                SB["cumt"][:, noff:noff + 1], OP.add, OP.bypass))
            p.dob("vector", lambda v, S, n=n, noff=noff: v.tensor_copy(
                SB["carry"][:, n:n + 1], SB["cumt"][:, noff + TT:noff + TT + 1]))

    def emit_transposes(src, dests, act=None, spill_gc=None):
        for n in range(KC):
            for ch in range(2):
                pb = PS[2 + (n * 2 + ch) % 2]
                p.wait_all("tensor", ["vector", "scalar"])
                tT = p.dob("tensor", lambda t, S, n=n, ch=ch, pb=pb, src=src: t.transpose(
                    pb[:, 0:128], SB[src][:, n * TT + ch * 128:n * TT + ch * 128 + 128],
                    SB["idents"][:]))
                if act is None:
                    p.wait("vector", "p_tensor", tT)
                    p.dob("vector", lambda v, S, n=n, ch=ch, pb=pb, dests=dests: v.tensor_copy(
                        SB[dests[ch]][:, n * 128:(n + 1) * 128], pb[:, 0:128]))
                else:
                    p.wait("scalar", "p_tensor", tT)
                    p.dob("scalar", lambda s, S, n=n, ch=ch, pb=pb, dests=dests, act=act: s.activation(
                        SB[dests[ch]][:, n * 128:(n + 1) * 128], pb[:, 0:128], act))

    def emit_g(tt):
        for ch in range(2):
            gc = tt * 2 + ch
            for n in range(KC):
                pb = PS[2 + n % 2]
                rs = "r%d" % (n % 4)
                p.wait_all("tensor", ["scalar", "gpsimd", "vector"])
                tT = p.dob("tensor", lambda t, S, n=n, ch=ch, pb=pb: t.transpose(
                    pb[:, 0:128], SB["vg_fm"][:, n * TT + ch * 128:n * TT + ch * 128 + 128],
                    SB["idents"][:]))
                p.wait("scalar", "p_tensor", tT)
                p.wait("scalar", rs, p.tok(rs))
                tA = p.dob("scalar", lambda s, S, n=n, pb=pb: s.activation(
                    SB["g_tm"][:, (n % 4) * 128:(n % 4) * 128 + 128], pb[:, 0:128], AF.Silu))
                p.wait("gpsimd", "p_scalar", tA)
                dma("gpsimd", rs,
                    lambda gc=gc, n=n: D["sc_g"][gc, :, n * 128:(n + 1) * 128],
                    lambda n=n: SB["g_tm"][:, (n % 4) * 128:(n % 4) * 128 + 128])

    def emit_scan_chunk(tt, ch):
        gc = tt * 2 + ch
        vt = "v_tm0" if ch == 0 else "v_tm1"
        rc_toks, y0_toks = [], []
        tT = 0
        for kc in range(KC):
            koff = kc * 257
            co = koff + 128 * ch
            p.wait_all("vector", ["scalar", "tensor"])
            for jn in range(3):
                inc = [("p_vector", 1)] if jn == 2 else None
                toksn = p.do("vector", lambda v, S, co=co, jn=jn: v.tensor_scalar(
                    SB["negs"][:, jn:jn + 1], SB["cumt"][:, co + 64 * jn:co + 64 * jn + 1],
                    -1.0, None, OP.mult), inc=inc)
            tV = toksn["p_vector"]
            p.wait("scalar", "p_vector", tV)
            p.do("scalar", lambda s, S, co=co: s.activation(
                SB["Er"][:, 0:64], SB["cumt"][:, co:co + 64], AF.Exp,
                bias=SB["cumt"][:, co:co + 1], scale=-1.0))
            p.do("scalar", lambda s, S, co=co: s.activation(
                SB["Er"][:, 64:128], SB["cumt"][:, co + 64:co + 128], AF.Exp,
                bias=SB["cumt"][:, co + 64:co + 65], scale=-1.0))
            p.do("scalar", lambda s, S, co=co: s.activation(
                SB["Ek"][:, 0:64], SB["cumt"][:, co + 1:co + 65], AF.Exp,
                bias=SB["negs"][:, 0:1], scale=1.0))
            p.do("scalar", lambda s, S, co=co: s.activation(
                SB["Ek"][:, 64:128], SB["cumt"][:, co + 65:co + 129], AF.Exp,
                bias=SB["negs"][:, 1:2], scale=1.0))
            p.do("scalar", lambda s, S, co=co: s.activation(
                SB["Eh"][:], SB["cumt"][:, co + 1:co + 129], AF.Exp,
                bias=SB["negs"][:, 2:3], scale=1.0))
            p.do("scalar", lambda s, S, co=co: s.activation(
                SB["emid"][:], SB["cumt"][:, co:co + 1], AF.Exp, bias=SB["negs"][:, 1:2], scale=1.0))
            tA = p.dob("scalar", lambda s, S, co=co: s.activation(
                SB["pch"][:], SB["cumt"][:, co:co + 1], AF.Exp, bias=SB["negs"][:, 2:3], scale=1.0))
            p.wait("vector", "p_scalar", tA)
            roff = kc * TT + 128 * ch
            p.do("vector", lambda v, S, roff=roff: v.tensor_tensor(
                SB["rt_t"][:], SB["r_fm"][:, roff:roff + 128], SB["Er"][:], OP.mult))
            p.do("vector", lambda v, S, roff=roff: v.tensor_tensor(
                SB["kt_t"][:], SB["k_fm"][:, roff:roff + 128], SB["Ek"][:], OP.mult))
            p.do("vector", lambda v, S, roff=roff: v.tensor_tensor(
                SB["kh_t"][:], SB["k_fm"][:, roff:roff + 128], SB["Eh"][:], OP.mult))
            p.do("vector", lambda v, S: v.tensor_scalar(
                SB["rmid"][:], SB["rt_t"][:, 64:128], SB["emid"][:], None, OP.mult))
            p.do("vector", lambda v, S, roff=roff, kc=kc: v.scalar_tensor_tensor(
                SB["ruk"][:], SB["r_fm"][:, roff:roff + 128], vcol("u", kc),
                SB["k_fm"][:, roff:roff + 128], OP.mult, OP.mult))
            rs2 = "r%d" % (kc % 4)
            p.wait("vector", rs2, p.tok(rs2))
            rco = (kc % 4) * 128
            p.do("vector", lambda v, S, kc=kc, rco=rco: v.tensor_scalar(
                SB["rc_t"][:, rco:rco + 64], SB["rt_t"][:, 0:64],
                SB["pacc"][:, kc:kc + 1], None, OP.mult))
            p.do("vector", lambda v, S, kc=kc, rco=rco: v.tensor_scalar(
                SB["rc_t"][:, rco + 64:rco + 128], SB["rmid"][:],
                SB["pacc"][:, kc:kc + 1], None, OP.mult))
            tV = p.dob("vector", lambda v, S, kc=kc: v.tensor_tensor(
                SB["pacc"][:, kc:kc + 1], SB["pacc"][:, kc:kc + 1], SB["pch"][:], OP.mult))
            p.wait("gpsimd", "p_vector", tV)
            dma("gpsimd", rs2,
                lambda gc=gc, kc=kc: D["sc_rc"][gc, :, kc * 128:(kc + 1) * 128],
                lambda rco=rco: SB["rc_t"][:, rco:rco + 128])
            if scan_lvl < 2:
                continue
            if "1" not in scan_parts:
                p.wait("tensor", "p_vector", tV)
            psA = PS[kc % 2]
            psB = PS[2 + kc % 2]
            if "d" in scan_parts:
                p.do("tensor", lambda t, S, kc=kc: t.matmul(
                    PS[7][:, 2 * kc:2 * kc + 2], SB["ruk"][:], SB["dcols2s"][:],
                    start=True, stop=True))
            if "A" in scan_parts or "a" in scan_parts or "z" in scan_parts:
                combos = [(0, 0, 0, 0, 0), (0, 64, 64, 64, 0), (64, 0, 0, 0, 64), (64, 64, 64, 64, 64)]
                if "a" in scan_parts:
                    combos = combos[:1]
                elif "z" in scan_parts:
                    combos = combos[3:]
                for (ih, sh, th, orow, ocol) in combos:
                    p.do("tensor", lambda t, S, ih=ih, sh=sh, th=th, orow=orow, ocol=ocol, psA=psA: t.matmul(
                        psA[orow:orow + 64, ocol:ocol + 64], SB["kt_t"][ih:ih + 64, sh:sh + 64],
                        SB["rt_t"][ih:ih + 64, th:th + 64], start=True, stop=True,
                        tile_position=(ih, orow)))
            if "B" in scan_parts:
                for ih, ocol in [(0, 0), (64, 64)]:
                    p.do("tensor", lambda t, S, ih=ih, ocol=ocol, psB=psB: t.matmul(
                        psB[0:64, ocol:ocol + 64], SB["kt_t"][ih:ih + 64, 0:64],
                        SB["rmid"][ih:ih + 64, :], start=True, stop=True))
            if "T" in scan_parts:
                p.do("tensor", lambda t, S, psB=psB: t.transpose(
                    psB[:, 128:256], SB["kh_t"][:], SB["idents"][:]))
            tT = p.softmark("tensor") if "2" not in scan_parts else tT
            if "e" in scan_parts:
                p.wait("vector", "p_tensor", tT)
                p.do("vector", lambda v, S, psA=psA: v.tensor_tensor(
                    SB["A2m"][:], psA[:, 0:128], SB["mask4s"][:], OP.mult))
                p.do("vector", lambda v, S, psB=psB: v.tensor_copy(SB["Bm"][:], psB[0:64, 0:128]))
                p.do("vector", lambda v, S, psB=psB: v.tensor_copy(SB["khTm"][:], psB[:, 128:256]))
            if "3" not in scan_parts:
                tV = p.dob("vector", lambda v, S, kc=kc: v.tensor_copy(
                    SB["d_m"][:, 2 * kc:2 * kc + 2], PS[7][:, 2 * kc:2 * kc + 2]) if "e" in scan_parts
                    else v.tensor_copy(SB["d_m"][:, 0:2], SB["dcols2s"][:]))
            if scan_lvl < 3:
                continue
            p.wait("tensor", "p_vector", tV)
            psY = PS[4 + kc % 2]
            seq = []
            for h in range(2):
                ih = h * 64
                jc = kc * 128 + h * 64
                seq += [
                    ("A2m", 0, ih, vt, 0, jc, 0, ih),
                    ("rt_t", ih, 0, "Sloc", ih, kc * 64, 0, ih),
                ]
            for h in range(2):
                ih = h * 64
                jc = kc * 128 + h * 64
                seq += [
                    ("A2m", 64, ih, vt, 64, jc, 64, ih),
                    ("Bm", 0, ih, vt, 0, jc, 64, ih),
                    ("rmid", ih, 0, "Sloc", ih, kc * 64, 64, ih),
                ]
            for q, (ln, lr, lc, rn, rr, rc2, orow, ocol) in enumerate(seq):
                p.do("tensor", lambda t, S, ln=ln, lr=lr, lc=lc, rn=rn, rr=rr, rc2=rc2,
                     orow=orow, ocol=ocol, q=q, psY=psY: t.matmul(
                    psY[orow:orow + 64, ocol:ocol + 64],
                    SB[ln][lr:lr + 64, lc:lc + 64], SB[rn][rr:rr + 64, rc2:rc2 + 64],
                    start=(q in (0, 4)), stop=(q in (3, 9)),
                    tile_position=(lr, orow)))
            psS = PS[6]
            for h in range(2):
                ih = h * 64
                jc = kc * 128 + h * 64
                inc = [("p_tensor", 1)] if h == 1 else None
                toks = p.do("tensor", lambda t, S, ih=ih, jc=jc, psS=psS, kc=kc, vt=vt: t.matmul(
                    psS[ih:ih + 64, (kc % 2) * 64:(kc % 2) * 64 + 64],
                    SB["khTm"][:, ih:ih + 64], SB[vt][:, jc:jc + 64],
                    start=True, stop=True, tile_position=(0, ih)), inc=inc)
            if scan_lvl < 4:
                continue
            p.wait("vector", "p_tensor", toks["p_tensor"])
            p.do("vector", lambda v, S, kc=kc, vt=vt: v.tensor_scalar(
                SB["dv"][:, 0:64], SB[vt][:, kc * 128:kc * 128 + 64],
                SB["d_m"][:, 2 * kc:2 * kc + 1], None, OP.mult))
            p.do("vector", lambda v, S, kc=kc, vt=vt: v.tensor_scalar(
                SB["dv"][:, 64:128], SB[vt][:, kc * 128 + 64:kc * 128 + 128],
                SB["d_m"][:, 2 * kc + 1:2 * kc + 2], None, OP.mult))
            p.do("vector", lambda v, S, rco=rco, psY=psY: v.tensor_tensor(
                SB["y0t"][:, rco:rco + 128], psY[:, 0:128], SB["dv"][:], OP.add))
            tS = p.dob("vector", lambda v, S, kc=kc, psS=psS: v.scalar_tensor_tensor(
                SB["Sloc"][:, kc * 64:(kc + 1) * 64], SB["Sloc"][:, kc * 64:(kc + 1) * 64],
                SB["pch"][:], psS[:, (kc % 2) * 64:(kc % 2) * 64 + 64], OP.mult, OP.add))
            p.wait("gpsimd", "p_vector", tS)
            dma("gpsimd", rs2,
                lambda gc=gc, kc=kc: D["sc_y0"][gc, :, kc * 128:(kc + 1) * 128],
                lambda rco=rco: SB["y0t"][:, rco:rco + 128])

    for tt in range(NTT if p3_ntt is None else p3_ntt):
        t0 = tt * TT
        p.wait_all("sync", ["vector", "scalar", "tensor"])
        for kc in range(KC):
            dma("sync", "dX", lambda kc=kc: SB["xn_tile"][:, kc * 257:(kc + 1) * 257],
                lambda kc=kc, t0=t0: D["sc_xn"][kc, :, t0:t0 + 257])
        t_xn = p.tok("dX")
        p.wait("vector", "dX", t_xn)
        for kc in range(KC):
            koff = kc * 257
            xxx = "xxx_a" if kc % 2 == 0 else "xxx_b"
            p.wait_all("vector", ["tensor"])
            p.do("vector", lambda v, S, koff=koff, xxx=xxx: v.tensor_tensor(
                SB[xxx][:], SB["xn_tile"][:, koff:koff + TT],
                SB["xn_tile"][:, koff + 1:koff + 1 + TT], OP.subtract))
            tV = p.dob("vector", lambda v, S, kc=kc, xxx=xxx, koff=koff: v.scalar_tensor_tensor(
                SB[xxx][:], SB[xxx][:], vcol("x", kc),
                SB["xn_tile"][:, koff + 1:koff + 1 + TT], OP.mult, OP.add))
            p.wait("tensor", "p_vector", tV)
            p.do("tensor", lambda t, S, kc=kc, xxx=xxx: t.matmul(
                PS[0][:, 0:TT], SB["w1s"][:, kc * 160:kc * 160 + 128], SB[xxx][:],
                start=(kc == 0), stop=(kc == KC - 1)))
            inc = [("p_tensor", 1)] if kc == KC - 1 else None
            toks = p.do("tensor", lambda t, S, kc=kc, xxx=xxx: t.matmul(
                PS[1][0:32, 0:TT], SB["w1s"][:, kc * 160 + 128:kc * 160 + 160], SB[xxx][:],
                start=(kc == 0), stop=(kc == KC - 1)), inc=inc)
        p.wait("scalar", "p_tensor", toks["p_tensor"])
        p.do("scalar", lambda s, S: s.activation(SB["m_s"][:], PS[0][:, 0:TT], AF.Tanh))
        tA = p.dob("scalar", lambda s, S: s.activation(SB["m4_s"][:], PS[1][0:32, 0:TT], AF.Tanh))
        p.wait("tensor", "p_scalar", tA)
        p.wait("vector", "p_scalar", tA)

        if not p3_skip_proj:
            xi = ["xinp_a", "xinp_b"]
            for pi, (pn, fam, wname) in enumerate(PROJ):
                xinp = xi[pi % 2]
                emit_mix_and_xinput(fam, xinp, pn)
                if pn == "w":
                    emit_rate(xinp)
                elif pn == "r":
                    emit_proj_mms(wname, xinp, "r_fm")
                elif pn == "k":
                    emit_proj_mms(wname, xinp, "k_fm")
                elif pn == "v":
                    emit_proj_mms(wname, xinp, "vg_fm")
                    emit_transposes("vg_fm", ("v_tm0", "v_tm1"))
                else:
                    emit_proj_mms(wname, xinp, "vg_fm")
                    emit_g(tt)

        if not (p3_skip_scan or p3_skip_proj):
            for ch in range(2):
                emit_scan_chunk(tt, ch)
        p.barrier()

    for s in ("r0", "r1", "r2", "r3"):
        p.wait("gpsimd", s, p.tok(s))
    p.wait("gpsimd", "dG", p.tok("dG"))
    p.barrier()

    if stop_after == "P3":
        dma("gpsimd", "dG", lambda: D["sc_y0"][0, :, :], lambda: SB["r_fm"][:, 0:2048])
        dma("gpsimd", "dG", lambda: D["sc_y0"][1, :, :], lambda: SB["r_fm"][:, 2048:4096])
        dma("gpsimd", "dG", lambda: D["sc_y0"][2, :, :], lambda: SB["k_fm"][:, 0:2048])
        dma("gpsimd", "dG", lambda: D["sc_y0"][3, :, :], lambda: SB["k_fm"][:, 2048:4096])
        dma("gpsimd", "dG", lambda: D["sc_y0"][4, :, :], lambda: SB["v_tm0"][:])
        dma("gpsimd", "dG", lambda: D["sc_y0"][5, :, :], lambda: SB["v_tm1"][:])
        dma("gpsimd", "dG", lambda: D["sc_y0"][6, :, :], lambda: SB["cumt"][:, 0:2048])
        dma("gpsimd", "dG", lambda: D["sc_y0"][7, :, :], lambda: SB["cumt"][:, 2048:4096])
        dma("gpsimd", "dG", lambda: D["sc_rc"][0, :, 0:1024], lambda: SB["Sloc"][:])
        dma("gpsimd", "dG", lambda: D["sc_rc"][1, :, 0:16], lambda: SB["pacc"][:])
        p.barrier()
        es34.close()
        _emit_taps(nc, p, D, SB, taps, dma)
        _finalize(nc, p)
        es_all.close()
        return nc

    es34.close()

    sb(es_all, "Sin", [128, KC * 64])
    p.do("vector", lambda v, S: v.memset(SB["Sin"][:], 0.0))
    p.wait_all("gpsimd", ["vector"])
    for kc in range(KC):
        dma("gpsimd", "dG", lambda kc=kc: D["cc_in"][kc * 128:(kc + 1) * 128, :],
            lambda kc=kc: SB["Sloc"][:, kc * 64:(kc + 1) * 64])
    dma("gpsimd", "dG", lambda: D["cc_in"][2048:2176, 0:16], lambda: SB["pacc"][:])
    p.wait("gpsimd", "dG", p.tok("dG"))
    p.do("gpsimd", lambda g, S: g.collective_compute(
        "AllGather", OP.bypass, replica_groups=[list(range(NCORES))],
        ins=[D["cc_in"][:]], outs=[D["cc_out"][:]]), inc=[("CC", 1)])
    p.wait("sync", "CC", 1)

    with contextlib.ExitStack() as es5:
        sb(es5, "ejt", [128, 64])
        sb(es5, "ejm", [128, 64]); sb(es5, "pj_all", [128, KC]); sb(es5, "pjtm", [128, KC])
        for j in range(NCORES):
            p.wait_all("sync", ["vector"])
            tp = dma("sync", "dS", lambda: SB["pj_all"][:],
                     lambda j=j: D["cc_out"][j * 2176 + 2048:j * 2176 + 2176, 0:16])
            p.wait("vector", "dS", tp)
            p.do("vector", lambda v, S, j=j: v.tensor_scalar(
                SB["pjtm"][:], SB["pj_all"][:], SB["lts"][:, j:j + 1],
                SB["omlts"][:, j:j + 1], OP.mult, OP.add))
            for kc in range(KC):
                p.wait_all("sync", ["vector"])
                tl = dma("sync", "dS", lambda: SB["ejt"][:],
                         lambda j=j, kc=kc: D["cc_out"][j * 2176 + kc * 128:j * 2176 + (kc + 1) * 128, :])
                p.wait("vector", "dS", tl)
                p.do("vector", lambda v, S, j=j: v.tensor_scalar(
                    SB["ejm"][:], SB["ejt"][:], SB["lts"][:, j:j + 1], None, OP.mult))
                p.dob("vector", lambda v, S, kc=kc: v.scalar_tensor_tensor(
                    SB["Sin"][:, kc * 64:(kc + 1) * 64], SB["Sin"][:, kc * 64:(kc + 1) * 64],
                    SB["pjtm"][:, kc:kc + 1], SB["ejm"][:], OP.mult, OP.add))
    p.barrier()

    if stop_after == "P5":
        dma("gpsimd", "dG", lambda: D["sc_y0"][0, :, 0:1024], lambda: SB["Sin"][:])
        p.barrier()
        _emit_taps(nc, p, D, SB, taps, dma)
        _finalize(nc, p)
        es_all.close()
        return nc

    es6 = contextlib.ExitStack()
    sb(es6, "y0r0", [128, C]); sb(es6, "y0r1", [128, C])
    sb(es6, "gr0", [128, C]); sb(es6, "gr1", [128, C])
    sb(es6, "rcr0", [128, C]); sb(es6, "rcr1", [128, C])
    sb(es6, "lnwbs", [128, C]); sb(es6, "lnbbs", [128, C])
    sb(es6, "g1t", [128, C]); sb(es6, "g2t", [128, C])
    sb(es6, "qt", [128, C]); sb(es6, "qfm", [128, C])
    sb(es6, "wopan_a", [128, C]); sb(es6, "wopan_b", [128, C])
    sb(es6, "s_t", [128, C]); sb(es6, "trash6", [128, 64])
    sb(es6, "gn_st", [128, 128])

    dma("sync", "dS", lambda: SB["lnwbs"][:], lambda: D["lnw_bc"][:])
    dma("sync", "dS", lambda: SB["lnbbs"][:], lambda: D["lnb_bc"][:])

    for tt in range(NTT):
        for ch in range(2):
            gc = tt * 2 + ch
            y0r = "y0r0" if ch == 0 else "y0r1"
            gr = "gr0" if ch == 0 else "gr1"
            rcr = "rcr0" if ch == 0 else "rcr1"
            p.wait_all("sync", ["vector", "tensor"])
            dma("sync", "dX", lambda y0r=y0r: SB[y0r][:], lambda gc=gc: D["sc_y0"][gc, :, :])
            dma("sync", "dX", lambda gr=gr: SB[gr][:], lambda gc=gc: D["sc_g"][gc, :, :])
            t3 = dma("sync", "dX", lambda rcr=rcr: SB[rcr][:], lambda gc=gc: D["sc_rc"][gc, :, :])
            p.wait("tensor", "dX", t3)
            for kc in range(KC):
                pb = PS[kc % 2]
                p.wait_all("tensor", ["vector"])
                for q, (ih, toff, orow) in enumerate([(0, 0, 0), (64, 0, 0), (0, 64, 64), (64, 64, 64)]):
                    inc = [("p_tensor", 1)] if q == 3 else None
                    toks = p.do("tensor", lambda t, S, kc=kc, pb=pb, ih=ih, toff=toff, orow=orow,
                                rcr=rcr, q=q: t.matmul(
                        pb[orow:orow + 64, ih:ih + 64],
                        SB[rcr][ih:ih + 64, kc * 128 + toff:kc * 128 + toff + 64],
                        SB["Sin"][ih:ih + 64, kc * 64:(kc + 1) * 64],
                        start=(q in (0, 2)), stop=(q in (1, 3)), tile_position=(ih, orow)), inc=inc)
                p.wait("vector", "p_tensor", toks["p_tensor"])
                p.wait("vector", "dX", t3)
                p.dob("vector", lambda v, S, kc=kc, pb=pb, y0r=y0r: v.tensor_tensor(
                    SB[y0r][:, kc * 128:(kc + 1) * 128], SB[y0r][:, kc * 128:(kc + 1) * 128],
                    pb[:, 0:128], OP.add))
            for h in range(32):
                p.do("vector", lambda v, S, h=h, y0r=y0r: v.reduce_sum(
                    SB["gn_st"][:, h:h + 1], SB[y0r][:, h * 64:(h + 1) * 64], axis=AX.X))
                p.do("vector", lambda v, S, h=h, y0r=y0r: v.tensor_tensor(
                    SB["trash6"][:], SB[y0r][:, h * 64:(h + 1) * 64],
                    SB[y0r][:, h * 64:(h + 1) * 64], OP.mult))
                p.do("vector", lambda v, S, h=h: v.reduce_sum(
                    SB["gn_st"][:, 32 + h:33 + h], SB["trash6"][:], axis=AX.X))
            p.do("vector", lambda v, S: v.tensor_scalar(
                SB["gn_st"][:, 64:96], SB["gn_st"][:, 0:32], 1.0 / 64, None, OP.mult))
            p.do("vector", lambda v, S: v.tensor_tensor(
                SB["gn_st"][:, 96:128], SB["gn_st"][:, 64:96], SB["gn_st"][:, 64:96], OP.mult))
            p.do("vector", lambda v, S: v.tensor_scalar(
                SB["gn_st"][:, 0:32], SB["gn_st"][:, 32:64], 1.0 / 64, None, OP.mult))
            tV = p.dob("vector", lambda v, S: v.tensor_tensor(
                SB["gn_st"][:, 0:32], SB["gn_st"][:, 0:32], SB["gn_st"][:, 96:128], OP.subtract))
            p.wait("scalar", "p_vector", tV)
            tA = p.dob("scalar", lambda s, S: s.activation(
                SB["gn_st"][:, 32:64], SB["gn_st"][:, 0:32], AF.Sqrt, bias=SB["epsss"][:, 1:2], scale=1.0))
            p.wait("vector", "p_scalar", tA)
            p.do("vector", lambda v, S: v.reciprocal(SB["gn_st"][:, 0:32], SB["gn_st"][:, 32:64]))
            p.do("vector", lambda v, S, gr=gr: v.tensor_tensor(SB["g1t"][:], SB[gr][:], SB["lnwbs"][:], OP.mult))
            p.do("vector", lambda v, S, gr=gr: v.tensor_tensor(SB["g2t"][:], SB[gr][:], SB["lnbbs"][:], OP.mult))
            for h in range(32):
                p.do("vector", lambda v, S, h=h, y0r=y0r: v.tensor_scalar(
                    SB["qt"][:, h * 64:(h + 1) * 64], SB[y0r][:, h * 64:(h + 1) * 64],
                    SB["gn_st"][:, 64 + h:65 + h], SB["gn_st"][:, h:h + 1], OP.subtract, OP.mult))
            p.do("vector", lambda v, S: v.tensor_tensor(SB["qt"][:], SB["qt"][:], SB["g1t"][:], OP.mult))
            tV = p.dob("vector", lambda v, S: v.tensor_tensor(SB["qt"][:], SB["qt"][:], SB["g2t"][:], OP.add))
            p.wait("tensor", "p_vector", tV)
            for kc in range(KC):
                pb = PS[2 + kc % 2]
                p.wait_all("tensor", ["vector"])
                tT = p.dob("tensor", lambda t, S, kc=kc, pb=pb: t.transpose(
                    pb[:, 0:128], SB["qt"][:, kc * 128:(kc + 1) * 128], SB["idents"][:]))
                p.wait("vector", "p_tensor", tT)
                p.dob("vector", lambda v, S, kc=kc, pb=pb: v.tensor_copy(
                    SB["qfm"][:, kc * 128:(kc + 1) * 128], pb[:, 0:128]))
            for kc in range(KC):
                pan = "wopan_a" if kc % 2 == 0 else "wopan_b"
                pnm = "pn%d" % (kc % 2)
                p.wait_all("sync", ["tensor"])
                tl = dma("sync", pnm, lambda pan=pan: SB[pan][:],
                         lambda kc=kc: D["wo"][kc * 128:(kc + 1) * 128, :])
                p.wait("tensor", pnm, tl)
                if kc == 0:
                    p.wait_all("tensor", ["vector"])
                for n in range(4):
                    inc = [("p_tensor", 1)] if (kc == KC - 1 and n == 3) else None
                    toks = p.do("tensor", lambda t, S, kc=kc, n=n, pan=pan: t.matmul(
                        PS[4 + n][:, 0:512], SB["qfm"][:, kc * 128:(kc + 1) * 128],
                        SB[pan][:, n * 512:(n + 1) * 512],
                        start=(kc == 0), stop=(kc == KC - 1)), inc=inc)
            p.wait("vector", "p_tensor", toks["p_tensor"])
            for n in range(4):
                p.dob("vector", lambda v, S, n=n: v.tensor_copy(
                    SB["s_t"][:, n * 512:(n + 1) * 512], PS[4 + n][:, 0:512]))
            p.wait_all("gpsimd", ["vector"])
            dma("gpsimd", "dG", lambda gc=gc: D["sc_s"][gc, :, :], lambda: SB["s_t"][:])
            p.wait("vector", "dG", p.tok("dG"))
    p.barrier()
    es6.close()

    if stop_after == "P6":
        _emit_taps(nc, p, D, SB, taps, dma)
        _finalize(nc, p)
        es_all.close()
        return nc

    with contextlib.ExitStack() as esq:
        sb(esq, "fpan", [128, C]); sb(esq, "fpan2", [128, C])
        sb(esq, "gacc", [128, 144])
        sb(esq, "thr", [128, 3]); sb(esq, "nthr", [128, 3]); sb(esq, "gs3", [128, 3])
        sb(esq, "qb_t", [128, C]); sb(esq, "qpan", [128, C], F16)
        sb(esq, "trash7", [128, C])

        pieces = []
        for r in range(44):
            pieces.append((0, "fw1", r, "sc_qw1", None))
        for r in range(44):
            pieces.append((1, "fw2", r, "sc_qw2", None))
        for r in range(16):
            for jj in range(3):
                pieces.append((2, "fw3", r, "sc_qw3", jj))

        def piece_w(pc):
            return 2048 if (pc[4] is None or pc[4] < 2) else MF - 4096

        def piece_src(pc):
            mi, dn, r, scn, jj = pc
            if jj is None:
                return lambda: D[dn][r * 128:(r + 1) * 128, :]
            w = piece_w(pc)
            return lambda: D[dn][r * 128:(r + 1) * 128, jj * 2048:jj * 2048 + w]

        for ip, pc in enumerate(pieces):
            pan = "fpan" if ip % 2 == 0 else "fpan2"
            pnm = "pn%d" % (ip % 2)
            w = piece_w(pc)
            p.wait_all("sync", ["scalar", "vector"])
            tl = dma("sync", pnm, lambda pan=pan, w=w: SB[pan][:, 0:w], piece_src(pc))
            p.wait("scalar", pnm, tl)
            tA = p.dob("scalar", lambda s, S, pan=pan, w=w: s.activation(
                SB["trash7"][:, 0:w], SB[pan][:, 0:w], AF.Abs))
            p.wait("vector", "p_scalar", tA)
            p.dob("vector", lambda v, S, ip=ip, w=w: v.reduce_sum(
                SB["gacc"][:, ip:ip + 1], SB["trash7"][:, 0:w], axis=AX.X))
        tA = p.tok("p_vector")
        p.wait("vector", "p_vector", 0)
        p.do("vector", lambda v, S: v.reduce_sum(SB["gs3"][:, 0:1], SB["gacc"][:, 0:44], axis=AX.X))
        p.do("vector", lambda v, S: v.reduce_sum(SB["gs3"][:, 1:2], SB["gacc"][:, 44:88], axis=AX.X))
        tV = p.dob("vector", lambda v, S: v.reduce_sum(SB["gs3"][:, 2:3], SB["gacc"][:, 88:136], axis=AX.X))
        p.wait("tensor", "p_vector", tV)
        tT = p.dob("tensor", lambda t, S: t.matmul(
            PS[0][:, 0:3], SB["ones128s"][:], SB["gs3"][:], start=True, stop=True))
        p.wait("vector", "p_tensor", tT)
        p.do("vector", lambda v, S: v.tensor_scalar(
            SB["thr"][:], PS[0][:, 0:3], 0.5 / (MF * C), 0.5 * FFN_EPS, OP.mult, OP.add))
        tV = p.dob("vector", lambda v, S: v.tensor_scalar(SB["nthr"][:], SB["thr"][:], -1.0, None, OP.mult))
        for ip, pc in enumerate(pieces):
            mi, dn, r, scn, jj = pc
            pan = "fpan" if ip % 2 == 0 else "fpan2"
            pnm = "pn%d" % (ip % 2)
            w = piece_w(pc)
            p.wait_all("sync", ["vector"])
            tl = dma("sync", pnm, lambda pan=pan, w=w: SB[pan][:, 0:w], piece_src(pc))
            p.wait("vector", pnm, tl)
            p.do("vector", lambda v, S, pan=pan, mi=mi, w=w: v.tensor_scalar(
                SB["qb_t"][:, 0:w], SB[pan][:, 0:w], SB["nthr"][:, mi:mi + 1], None, OP.is_lt))
            tq = p.dob("vector", lambda v, S, pan=pan, mi=mi, w=w: v.scalar_tensor_tensor(
                SB["qpan"][:, 0:w], SB[pan][:, 0:w], SB["thr"][:, mi:mi + 1],
                SB["qb_t"][:, 0:w], OP.is_gt, OP.subtract))
            p.wait("gpsimd", "p_vector", tq)
            if jj is None:
                tg = dma("gpsimd", "dG", lambda scn=scn, r=r: D[scn][r * 128:(r + 1) * 128, :],
                         lambda: SB["qpan"][:, 0:2048])
            else:
                tg = dma("gpsimd", "dG",
                         lambda scn=scn, r=r, jj=jj, w=w: D[scn][r * 128:(r + 1) * 128, jj * 2048:jj * 2048 + w],
                         lambda w=w: SB["qpan"][:, 0:w])
            p.wait("vector", "dG", tg)
    p.barrier()

    es7 = contextlib.ExitStack()
    sb(es7, "f_sb", [128, 2 * C])
    sb(es7, "xn7", [128, FT])
    sb(es7, "xnh", [128, KC * FT], F16); sb(es7, "xnl", [128, KC * FT], F16)
    sb(es7, "dkt", [128, FT]); sb(es7, "dku", [128, FT]); sb(es7, "dkh32", [128, FT])
    sb(es7, "q1g", [128, KC * 1024], F16)
    sb(es7, "q2g", [128, KC * 1024], F16)
    sb(es7, "q3g", [128, 8 * C], F16)
    sb(es7, "sig_t", [128, FT]); sb(es7, "hid32", [128, FT])
    sb(es7, "hidh", [128, 8 * FT], F16); sb(es7, "hidl", [128, 8 * FT], F16)
    sb(es7, "gf_t", [128, C]); sb(es7, "xre", [128, C]); sb(es7, "sre", [128, C]); sb(es7, "o_t", [128, C])

    MGRPS = [(0, 1024), (1024, 1024), (2048, 1024), (3072, 1024), (4096, 1024), (5120, 512)]

    for ft in range(NFT):
        t0 = ft * FT
        for kc in range(KC):
            p.wait_all("sync", ["vector"])
            tl = dma("sync", "dX", lambda: SB["xn7"][:],
                     lambda kc=kc, t0=t0: D["sc_xn"][kc, :, t0 + 1:t0 + 1 + FT])
            p.wait("vector", "dX", tl)
            p.do("vector", lambda v, S: v.tensor_scalar(SB["dkt"][:], SB["xn7"][:], 4097.0, None, OP.mult))
            p.do("vector", lambda v, S: v.tensor_tensor(SB["dku"][:], SB["dkt"][:], SB["xn7"][:], OP.subtract))
            p.do("vector", lambda v, S: v.tensor_tensor(SB["dkh32"][:], SB["dkt"][:], SB["dku"][:], OP.subtract))
            p.do("vector", lambda v, S, kc=kc: v.tensor_copy(SB["xnh"][:, kc * FT:(kc + 1) * FT], SB["dkh32"][:]))
            p.dob("vector", lambda v, S, kc=kc: v.tensor_tensor(
                SB["xnl"][:, kc * FT:(kc + 1) * FT], SB["xn7"][:], SB["dkh32"][:], OP.subtract))
        p.wait_all("vector", ["scalar"])
        tV = p.dob("vector", lambda v, S: v.memset(SB["f_sb"][:], 0.0))
        p.wait("tensor", "p_vector", tV)

        for (m0, mw) in MGRPS:
            nmc = mw // 128
            p.wait_all("sync", ["tensor"])
            for kc in range(KC):
                dma("sync", "dS", lambda kc=kc, mw=mw: SB["q1g"][:, kc * 1024:kc * 1024 + mw],
                    lambda kc=kc, m0=m0, mw=mw: D["sc_qw1"][m0:m0 + mw, kc * 128:(kc + 1) * 128],
                    transpose=True)
                dma("sync", "dS", lambda kc=kc, mw=mw: SB["q2g"][:, kc * 1024:kc * 1024 + mw],
                    lambda kc=kc, m0=m0, mw=mw: D["sc_qw2"][m0:m0 + mw, kc * 128:(kc + 1) * 128],
                    transpose=True)
            t_q12 = p.tok("dS")
            p.wait("tensor", "dS", t_q12)
            for mc in range(nmc):
                ps1 = PS[(mc % 2) * 2]
                ps2 = PS[(mc % 2) * 2 + 1]
                p.wait_all("tensor", ["vector", "scalar"])
                for kc in range(KC):
                    for hl, xb in ((0, "xnh"), (1, "xnl")):
                        p.do("tensor", lambda t, S, kc=kc, mc=mc, xb=xb, ps1=ps1: t.matmul(
                            ps1[:, 0:FT], SB["q1g"][:, kc * 1024 + mc * 128:kc * 1024 + (mc + 1) * 128],
                            SB[xb][:, kc * FT:(kc + 1) * FT],
                            start=(kc == 0 and xb == "xnh"), stop=(kc == KC - 1 and xb == "xnl")))
                for kc in range(KC):
                    for hl, xb in ((0, "xnh"), (1, "xnl")):
                        inc = [("p_tensor", 1)] if (kc == KC - 1 and hl == 1) else None
                        toks = p.do("tensor", lambda t, S, kc=kc, mc=mc, xb=xb, ps2=ps2: t.matmul(
                            ps2[:, 0:FT], SB["q2g"][:, kc * 1024 + mc * 128:kc * 1024 + (mc + 1) * 128],
                            SB[xb][:, kc * FT:(kc + 1) * FT],
                            start=(kc == 0 and xb == "xnh"), stop=(kc == KC - 1 and xb == "xnl")), inc=inc)
                p.wait("scalar", "p_tensor", toks["p_tensor"])
                tA = p.dob("scalar", lambda s, S, ps1=ps1: s.activation(SB["sig_t"][:], ps1[:, 0:FT], AF.Sigmoid))
                p.wait("vector", "p_scalar", tA)
                p.do("vector", lambda v, S, ps2=ps2: v.tensor_tensor(SB["hid32"][:], SB["sig_t"][:], ps2[:, 0:FT], OP.mult))
                p.do("vector", lambda v, S: v.tensor_scalar(SB["dkt"][:], SB["hid32"][:], 4097.0, None, OP.mult))
                p.do("vector", lambda v, S: v.tensor_tensor(SB["dku"][:], SB["dkt"][:], SB["hid32"][:], OP.subtract))
                p.do("vector", lambda v, S: v.tensor_tensor(SB["dkh32"][:], SB["dkt"][:], SB["dku"][:], OP.subtract))
                p.do("vector", lambda v, S, mc=mc: v.tensor_copy(SB["hidh"][:, mc * FT:(mc + 1) * FT], SB["dkh32"][:]))
                p.dob("vector", lambda v, S, mc=mc: v.tensor_tensor(
                    SB["hidl"][:, mc * FT:(mc + 1) * FT], SB["hid32"][:], SB["dkh32"][:], OP.subtract))
            p.wait_all("sync", ["tensor"])
            for mc in range(nmc):
                dma("sync", "dS", lambda mc=mc: SB["q3g"][:, mc * C:(mc + 1) * C],
                    lambda mc=mc, m0=m0: D["sc_qw3"][:, m0 + mc * 128:m0 + (mc + 1) * 128],
                    transpose=True)
            t_q3 = p.tok("dS")
            p.wait("tensor", "dS", t_q3)
            p.wait_all("tensor", ["vector"])
            for tsub in range(2):
                for n in range(4):
                    pb = PS[4 + (2 * tsub + n) % 4]
                    p.wait_all("tensor", ["vector"])
                    for mc in range(nmc):
                        for hl, hb in ((0, "hidh"), (1, "hidl")):
                            inc = [("p_tensor", 1)] if (mc == nmc - 1 and hl == 1) else None
                            toks = p.do("tensor", lambda t, S, tsub=tsub, n=n, mc=mc, hb=hb, pb=pb: t.matmul(
                                pb[:, 0:512], SB[hb][:, mc * FT + tsub * 128:mc * FT + tsub * 128 + 128],
                                SB["q3g"][:, mc * C + n * 512:mc * C + (n + 1) * 512],
                                start=(mc == 0 and hb == "hidh"), stop=(mc == nmc - 1 and hb == "hidl")), inc=inc)
                    p.wait("vector", "p_tensor", toks["p_tensor"])
                    p.dob("vector", lambda v, S, tsub=tsub, n=n, pb=pb: v.tensor_tensor(
                        SB["f_sb"][:, tsub * C + n * 512:tsub * C + (n + 1) * 512],
                        SB["f_sb"][:, tsub * C + n * 512:tsub * C + (n + 1) * 512],
                        pb[:, 0:512], OP.add))
        for tsub in range(2):
            gc = ft * 2 + tsub
            p.wait_all("scalar", ["vector"])
            tA = p.dob("scalar", lambda s, S, tsub=tsub: s.activation(
                SB["gf_t"][:], SB["f_sb"][:, tsub * C:(tsub + 1) * C], AF.Silu))
            p.wait_all("sync", ["vector"])
            t1 = dma("sync", "dS", lambda: SB["xre"][:],
                     lambda t0=t0, tsub=tsub: D["xin"][1 + t0 + tsub * 128:1 + t0 + (tsub + 1) * 128, :])
            t2 = dma("sync", "dS", lambda: SB["sre"][:], lambda gc=gc: D["sc_s"][gc, :, :])
            p.wait("vector", "p_scalar", tA)
            p.wait("vector", "dS", t2)
            p.do("vector", lambda v, S: v.tensor_tensor(SB["o_t"][:], SB["sre"][:], SB["gf_t"][:], OP.mult))
            tV = p.dob("vector", lambda v, S: v.tensor_tensor(SB["o_t"][:], SB["o_t"][:], SB["xre"][:], OP.add))
            p.wait("gpsimd", "p_vector", tV)
            dma("gpsimd", "dG",
                lambda t0=t0, tsub=tsub: D["outp"][t0 + tsub * 128:t0 + (tsub + 1) * 128, :],
                lambda: SB["o_t"][:])
    es7.close()

    _emit_taps(nc, p, D, SB, taps, dma)
    p.wait("gpsimd", "dG", p.tok("dG"))
    p.barrier()
    _finalize(nc, p)
    es_all.close()
    return nc


def _emit_taps(nc, p, D, SB, taps, dma):
    if not taps:
        return
    p.barrier()
    with contextlib.ExitStack() as est:
        bounce = est.enter_context(nc.sbuf_tensor("tapb", [128, 4096], F32))
        for name in taps:
            t = D[name]
            dims = list(t.shape)
            R = int(np.prod(dims[:-1]))
            Cc = dims[-1]
            out = nc.dram_tensor("tap_" + name, [R, Cc], t.dtype, kind="ExternalOutput").ap()
            flat = t.rearrange("a b c -> (a b) c") if len(dims) == 3 else t
            for i in range((R + 127) // 128):
                r0, r1 = i * 128, min((i + 1) * 128, R)
                tl = dma("sync", "dS", lambda r0=r0, r1=r1, Cc=Cc, bounce=bounce: bounce[0:r1 - r0, 0:Cc],
                         lambda flat=flat, r0=r0, r1=r1: flat[r0:r1, :])
                p.wait("gpsimd", "dS", tl)
                tg = dma("gpsimd", "dG", lambda out=out, r0=r0, r1=r1: out[r0:r1, :],
                         lambda r0=r0, r1=r1, Cc=Cc, bounce=bounce: bounce[0:r1 - r0, 0:Cc])
                p.wait("sync", "dG", tg)
        p.wait("gpsimd", "dG", p.tok("dG"))


def _finalize(nc, p):
    with contextlib.ExitStack() as es2:
        S = {}
        for nm in p.semnames:
            S[nm] = es2.enter_context(nc.semaphore(nm))
        block = es2.enter_context(nc.Block())

        def mk(eng_name):
            def body(e):
                for fn in p.q[eng_name]:
                    fn(e, S)
            return body
        block.sync(mk("sync"))
        block.tensor(mk("tensor"))
        block.vector(mk("vector"))
        block.scalar(mk("scalar"))
        block.gpsimd(mk("gpsimd"))
    return nc


def kernel(**inputs):
    in_maps = host_inputs(inputs)
    nc = build_program(taps=_BUILD_OPTS["taps"], stop_after=_BUILD_OPTS["stop_after"],
                       p3_skip_proj=_BUILD_OPTS["p3_skip_proj"],
                       p3_skip_scan=_BUILD_OPTS["p3_skip_scan"],
                       p3_ntt=_BUILD_OPTS["p3_ntt"], scan_lvl=_BUILD_OPTS["scan_lvl"])
    res = run_bass_kernel_spmd(nc, in_maps, core_ids=list(range(NCORES)))
    out = np.concatenate([np.asarray(r["outp"]) for r in res.results], axis=0)
    if _BUILD_OPTS["taps"] or _BUILD_OPTS["stop_after"]:
        kernel._tap_results = res.results
    return out.reshape(1, NCORES * TC, C)


# version 10
# speedup vs baseline: 1.0054x; 1.0054x over previous
import contextlib
import numpy as np
import concourse.bass as bass
import concourse.mybir as mybir
from concourse.bass_utils import run_bass_kernel_spmd

F32 = mybir.dt.float32
F16 = mybir.dt.float16
OP = mybir.AluOpType
AF = mybir.ActivationFunctionType
AX = mybir.AxisListType

NCORES = 8
TC = 1024
C = 2048
KC = 16
NH = 64
CH = 128
NCH = TC // CH
TT = 256
NTT = TC // TT
MF = 5632
FT = 256
NFT = TC // FT
LN_EPS = 1e-5
GN_EPS = 1e-5 * 64.0
FFN_EPS = 1e-8

ENGINES = ("sync", "tensor", "vector", "scalar", "gpsimd")

_BUILD_OPTS = {"taps": (), "stop_after": None, "p3_skip_proj": False,
               "p3_skip_scan": False, "p3_ntt": None, "scan_lvl": 4,
               "scan_parts": "dABTe"}


class Prog:
    def __init__(self):
        self.q = {e: [] for e in ENGINES}
        self.cnt = {}
        self.semnames = []

    def sem(self, name):
        self.semnames.append(name)
        self.cnt[name] = 0

    def do(self, eng, fn, inc=None):
        nm = "p_" + eng
        incs = list(inc or [])
        if not incs:
            incs.append((nm, 1))
        if any(n == nm for n, _ in incs):
            self.wait(eng, nm, self.cnt[nm])
        for nm2, k in incs:
            self.cnt[nm2] += k
        toks = {nm2: self.cnt[nm2] for nm2, _ in incs}

        def run(e, S, fn=fn, incs=incs):
            r = fn(e, S)
            for nm2, k in incs:
                r = r.then_inc(S[nm2], k)
        self.q[eng].append(run)
        return toks

    def dob(self, eng, fn):
        nm = "p_" + eng
        return self.do(eng, fn, inc=[(nm, 1)])[nm]

    def wait(self, eng, nm, val):
        if val <= 0:
            return

        def run(e, S, nm=nm, val=val):
            e.wait_ge(S[nm], val)
        self.q[eng].append(run)

    def tok(self, nm):
        return self.cnt[nm]

    def wait_all(self, eng, others):
        for o in others:
            nm = o if (o.startswith("d") or o == "CC") else "p_" + o
            self.wait(eng, nm, self.cnt[nm])

    def mark(self, eng):
        nm = "p_" + eng
        self.cnt[nm] += 1
        val = self.cnt[nm]

        def run(e, S, nm=nm):
            e.drain()
            e.sem_inc(S[nm], 1)
        self.q[eng].append(run)
        return val

    def softmark(self, eng):
        nm = "p_" + eng
        self.cnt[nm] += 1
        val = self.cnt[nm]

        def run(e, S, nm=nm):
            e.sem_inc(S[nm], 1)
        self.q[eng].append(run)
        return val

    def barrier(self, engines=ENGINES):
        toks = {eng: self.mark(eng) for eng in engines}
        for eng in engines:
            for other in engines:
                if other != eng:
                    self.wait(eng, "p_" + other, toks[other])


def host_inputs(inputs):
    f = lambda k: np.ascontiguousarray(np.asarray(inputs[k], dtype=np.float32))
    x = f("L0_attn_x")[0]
    wr, wk, wv, wg, wo = (f("L0_attn_w" + n) for n in ("r", "k", "v", "g", "o"))
    w1 = f("L0_attn_w1")
    w2 = np.ascontiguousarray(f("L0_attn_w2").reshape(160, 2048))
    tdw1, tdw2 = f("L0_attn_td_w1"), f("L0_attn_td_w2")
    fw1, fw2, fw3 = f("L0_ffn_w1"), f("L0_ffn_w2"), f("L0_ffn_w3")

    def npanels(w):
        return np.ascontiguousarray(
            w.reshape(KC, 128, KC, 128).transpose(2, 1, 0, 3).reshape(KC, C, 128))

    def ppack(v):
        return np.ascontiguousarray(v.reshape(KC, 128).T)

    vec_cols = [ppack(f("L0_attn_time_maa_" + n)[0, 0]) for n in ("x", "w", "k", "v", "r", "g")]
    vec_cols.append(ppack(f("L0_attn_time_decay")[0, 0]))
    vec_cols.append(ppack(f("L0_attn_time_faaaa").reshape(C)))
    vecs = np.concatenate(vec_cols, axis=1)

    lnw_bc = np.ascontiguousarray(np.broadcast_to(f("L0_attn_ln_w"), (128, C)))
    lnb_bc = np.ascontiguousarray(np.broadcast_to(f("L0_attn_ln_b"), (128, C)))
    ident = np.eye(128, dtype=np.float32)
    ones128 = np.ones((128, 128), dtype=np.float32)
    pf, ff = np.arange(128)[:, None], np.arange(128)[None, :]
    mask4 = ((pf % 64) < (ff % 64)).astype(np.float32)
    dcols2 = np.zeros((128, 2), np.float32)
    dcols2[:64, 0] = 1.0
    dcols2[64:, 1] = 1.0

    epss = np.zeros((128, 2), np.float32)
    epss[:, 0] = LN_EPS
    epss[:, 1] = GN_EPS
    base = {
        "epss": epss,
        "wrp": npanels(wr), "wkp": npanels(wk), "wvp": npanels(wv), "wgp": npanels(wg),
        "wo": wo, "w1": w1, "w2f": w2, "tdw1": tdw1, "tdw2": tdw2,
        "vecs": vecs, "lnw_bc": lnw_bc, "lnb_bc": lnb_bc,
        "ident": ident, "ones128": ones128, "mask4": mask4, "dcols2": dcols2,
        "fw1": fw1, "fw2": fw2, "fw3": fw3,
    }
    in_maps = []
    for c in range(NCORES):
        xin = np.zeros((1032, C), np.float32)
        t0 = c * TC
        xin[1:1 + TC] = x[t0:t0 + TC]
        if c > 0:
            xin[0] = x[t0 - 1]
        lt = np.zeros((128, 8), np.float32)
        lt[:, :c] = 1.0
        m = dict(base)
        m["xin"] = xin
        m["lt"] = lt
        m["omlt"] = 1.0 - lt
        in_maps.append(m)
    return in_maps


def build_program(taps=(), stop_after=None, p3_skip_proj=False, p3_skip_scan=False, p3_ntt=None, scan_lvl=4,
                  scan_parts="dABTe"):
    nc = bass.Bass("TRN2", target_bir_lowering=False)
    p = Prog()

    D = {}
    def din(name, shape, dt=F32):
        D[name] = nc.dram_tensor(name, shape, dt, kind="ExternalInput").ap()
    def dout(name, shape, dt=F32):
        D[name] = nc.dram_tensor(name, shape, dt, kind="ExternalOutput").ap()
    def dint(name, shape, dt=F32):
        D[name] = nc.dram_tensor(name, shape, dt).ap()

    din("xin", [1032, C])
    for n in ("wrp", "wkp", "wvp", "wgp"):
        din(n, [KC, C, 128])
    din("wo", [C, C])
    din("w1", [C, 160]); din("w2f", [160, C])
    din("tdw1", [C, 64]); din("tdw2", [64, C])
    din("vecs", [128, 128]); din("lnw_bc", [128, C]); din("lnb_bc", [128, C])
    din("ident", [128, 128]); din("ones128", [128, 128]); din("mask4", [128, 128])
    din("dcols2", [128, 2]); din("lt", [128, 8]); din("omlt", [128, 8]); din("epss", [128, 2])
    din("fw1", [MF, C]); din("fw2", [MF, C]); din("fw3", [C, MF])
    dout("outp", [TC, C])
    dint("sc_xn", [KC, 128, 1032])
    dint("sc_raw", [9, 128, C])
    dint("sc_y0", [NCH, 128, C])
    dint("sc_rc", [NCH, 128, C])
    dint("sc_g", [NCH, 128, C])
    dint("sc_s", [NCH, 128, C])
    dint("sc_qw1", [MF, C], F16)
    dint("sc_qw2", [MF, C], F16)
    dint("sc_qw3", [C, MF], F16)
    dint("cc_in", [2176, 64])
    dint("cc_out", [NCORES * 2176, 64])
    dint("dumA", [4096, C])
    dint("dumB", [4096, C])

    es_all = contextlib.ExitStack()
    PS = [es_all.enter_context(nc.psum_tensor(f"pb{i}", [128, 512], F32)) for i in range(8)]

    SB = {}
    def sb(es, name, shape, dt=F32):
        SB[name] = es.enter_context(nc.sbuf_tensor(name, shape, dt))
        return SB[name]

    sb(es_all, "idents", [128, 128])
    sb(es_all, "ones128s", [128, 128])
    sb(es_all, "mask4s", [128, 128])
    sb(es_all, "vecss", [128, 128])
    sb(es_all, "dcols2s", [128, 2]); sb(es_all, "epsss", [128, 2])
    sb(es_all, "lts", [128, 8]); sb(es_all, "omlts", [128, 8])
    sb(es_all, "g_st", [128, 16])
    sb(es_all, "carry", [128, KC]); sb(es_all, "pacc", [128, KC])
    sb(es_all, "negs", [128, 3]); sb(es_all, "emid", [128, 1]); sb(es_all, "pch", [128, 1])
    sb(es_all, "Sloc", [128, KC * 64])

    for e in ENGINES:
        p.sem("p_" + e)
    p.sem("dS"); p.sem("dG"); p.sem("dW"); p.sem("dX"); p.sem("CC")
    for s in ("lx0", "lx1", "rw0", "rw1", "pn0", "pn1", "r0", "r1", "r2", "r3"):
        p.sem(s)

    VOFF = {"x": 0, "w": 16, "k": 32, "v": 48, "r": 64, "g": 80, "tdec": 96, "u": 112}

    def vcol(n, kc):
        return SB["vecss"][:, VOFF[n] + kc:VOFF[n] + kc + 1]

    def dma(eng, sem, out_fn, in_fn, transpose=False):
        def fn(e, S):
            return e.dma_start(out=out_fn(), in_=in_fn(), transpose=transpose)
        return p.do(eng, fn, inc=[(sem, 16)])[sem]

    for dst, src in [("vecss", "vecs"), ("idents", "ident"), ("ones128s", "ones128"),
                     ("mask4s", "mask4"), ("dcols2s", "dcols2"), ("lts", "lt"), ("omlts", "omlt"), ("epsss", "epss")]:
        dma("sync", "dS", lambda dst=dst: SB[dst][:], lambda src=src: D[src][:])
    t_const = p.tok("dS")
    p.wait("vector", "dS", t_const)
    p.do("vector", lambda v, S: v.memset(SB["g_st"][:], 0.0))
    wv = [0]

    def wslot(n=32):
        c = wv[0]
        wv[0] += n
        return c

    for _ in range(2):
        p.do("vector", lambda v, S, c=wslot(): v.reduce_sum(SB["Sloc"][:, c:c + 1], SB["idents"][:], axis=AX.X))
        p.do("vector", lambda v, S, c=wslot(): v.tensor_tensor(SB["Sloc"][:, c:c + 32], SB["idents"][:, 0:32], SB["idents"][:, 0:32], OP.mult))
        p.do("vector", lambda v, S, c=wslot(): v.tensor_scalar(SB["Sloc"][:, c:c + 32], SB["idents"][:, 0:32], 2.0, None, OP.mult))
        p.do("vector", lambda v, S, c=wslot(): v.tensor_copy(SB["Sloc"][:, c:c + 32], SB["idents"][:, 0:32]))
        p.do("vector", lambda v, S, c=wslot(): v.reciprocal(SB["Sloc"][:, c:c + 1], SB["epsss"][:, 0:1]))
        p.do("vector", lambda v, S, c=wslot(): v.tensor_tensor_scan(SB["Sloc"][:, c:c + 32], SB["idents"][:, 0:32], SB["idents"][:, 0:32], 0.0, OP.add, OP.bypass))
        p.do("vector", lambda v, S, c=wslot(): v.scalar_tensor_tensor(SB["Sloc"][:, c:c + 32], SB["idents"][:, 0:32], 1.0, SB["idents"][:, 0:32], OP.mult, OP.add))
    t_wv = p.mark("vector")
    p.wait("scalar", "dS", t_const)
    for fn_ in (AF.Tanh, AF.Exp, AF.Sigmoid, AF.Silu, AF.Sqrt, AF.Square, AF.Abs, AF.Identity):
        p.do("scalar", lambda s, S, fn_=fn_, c=wslot(): s.activation(
            SB["Sloc"][:, c:c + 32], SB["mask4s"][:, 0:32], fn_))
    t_wa = p.mark("scalar")
    p.wait("tensor", "dS", t_const)
    for _ in range(4):
        p.do("tensor", lambda t, S: t.matmul(PS[0][:, 0:128], SB["idents"][:], SB["idents"][:], start=True, stop=True))
        p.do("tensor", lambda t, S: t.transpose(PS[1][:, 0:128], SB["idents"][:], SB["idents"][:]))
    t_wt = p.mark("tensor")
    p.wait("gpsimd", "dS", t_const)
    for wg in range(2):
        p.do("gpsimd", lambda g, S, c=wslot(): g.tensor_copy(
            SB["Sloc"][:, c:c + 32], SB["idents"][:, 0:32]))
        p.do("gpsimd", lambda g, S, c=wslot(): g.tensor_scalar(
            SB["Sloc"][:, c:c + 32], SB["idents"][:, 0:32], 1.0, None, OP.mult))
    t_wg = p.mark("gpsimd")
    for eng in ENGINES:
        p.wait(eng, "p_vector", t_wv)
        p.wait(eng, "p_scalar", t_wa)
        p.wait(eng, "p_tensor", t_wt)
        p.wait(eng, "p_gpsimd", t_wg)

    with contextlib.ExitStack() as es1:
        sb(es1, "xt0", [128, C]); sb(es1, "xt1", [128, C])
        sb(es1, "xnt", [128, C]); sb(es1, "stage", [128, C])
        sb(es1, "trash1", [128, C]); sb(es1, "statw", [128, 9 * 16])
        for it_ in range(18):
            ppass, i = divmod(it_, 9)
            slot = it_ % 2
            xt = "xt0" if slot == 0 else "xt1"
            lx, rw = "lx%d" % slot, "rw%d" % slot
            w = 128 if i < 8 else 8
            p.wait_all("sync", ["vector", "scalar"])
            p.wait("sync", rw, p.tok(rw))
            tl = dma("sync", lx, lambda xt=xt, w=w: SB[xt][0:w, :],
                     lambda i=i, w=w: D["xin"][i * 128:i * 128 + w, :])
            p.wait("gpsimd", lx, tl)
            dma("gpsimd", rw, lambda i=i, w=w: D["sc_raw"][i, 0:w, :],
                lambda xt=xt, w=w: SB[xt][0:w, :])
            p.wait("vector", lx, tl)
            p.do("vector", lambda v, S, xt=xt, w=w: v.reduce_sum(SB["g_st"][0:w, 0:1], SB[xt][0:w, :], axis=AX.X))
            p.do("vector", lambda v, S, xt=xt, w=w: v.tensor_tensor(
                SB["trash1"][0:w, :], SB[xt][0:w, :], SB[xt][0:w, :], OP.mult))
            p.do("vector", lambda v, S, w=w: v.reduce_sum(SB["g_st"][0:w, 1:2], SB["trash1"][0:w, :], axis=AX.X))
            p.do("vector", lambda v, S, w=w: v.tensor_scalar(SB["g_st"][0:w, 2:3], SB["g_st"][0:w, 0:1], 1.0 / C, None, OP.mult))
            p.do("vector", lambda v, S, w=w: v.tensor_tensor(SB["g_st"][0:w, 3:4], SB["g_st"][0:w, 2:3], SB["g_st"][0:w, 2:3], OP.mult))
            tV = p.dob("vector", lambda v, S, w=w: v.tensor_scalar(
                SB["g_st"][0:w, 4:5], SB["g_st"][0:w, 1:2], 1.0 / C, SB["g_st"][0:w, 3:4], OP.mult, OP.subtract))
            p.wait("scalar", "p_vector", tV)
            tA = p.dob("scalar", lambda s, S, w=w: s.activation(
                SB["g_st"][0:w, 8:9], SB["g_st"][0:w, 4:5], AF.Sqrt, bias=SB["epsss"][0:w, 0:1], scale=1.0))
            p.wait("vector", "p_scalar", tA)
            p.do("vector", lambda v, S, w=w: v.reciprocal(SB["g_st"][0:w, 12:13], SB["g_st"][0:w, 8:9]))
            p.do("vector", lambda v, S, i=i: v.tensor_copy(SB["statw"][:, i * 16:(i + 1) * 16], SB["g_st"][:, 0:16]))
            tV = p.dob("vector", lambda v, S, xt=xt, w=w: v.tensor_scalar(
                SB["xnt"][0:w, :], SB[xt][0:w, :], SB["g_st"][0:w, 2:3], SB["g_st"][0:w, 12:13], OP.subtract, OP.mult))
            p.wait("tensor", "p_vector", tV)
            p.wait("vector", "dG", p.tok("dG"))
            for kc in range(KC):
                pb = PS[kc % 2]
                p.wait_all("tensor", ["vector"])
                tT = p.dob("tensor", lambda t, S, kc=kc, pb=pb, w=w: t.transpose(
                    pb[:, 0:w], SB["xnt"][0:w, kc * 128:(kc + 1) * 128],
                    SB["idents"][0:w, 0:w]))
                p.wait("vector", "p_tensor", tT)
                tV = p.dob("vector", lambda v, S, kc=kc, pb=pb, w=w: v.tensor_copy(
                    SB["stage"][:, kc * 128:kc * 128 + w], pb[:, 0:w]))
            p.wait("gpsimd", "p_vector", tV)
            for kc in range(KC):
                dma("gpsimd", "dG", lambda kc=kc, i=i, w=w: D["sc_xn"][kc, :, i * 128:i * 128 + w],
                    lambda kc=kc, w=w: SB["stage"][:, kc * 128:kc * 128 + w])

    p.wait("gpsimd", "dG", p.tok("dG"))
    p.wait("gpsimd", "rw0", p.tok("rw0"))
    p.wait("gpsimd", "rw1", p.tok("rw1"))
    p.barrier()
    if stop_after == "P1":
        dma("gpsimd", "dG", lambda: D["sc_y0"][0, :, 0:144], lambda: SB["statw"][:])
        p.barrier()
        _emit_taps(nc, p, D, SB, taps, dma)
        _finalize(nc, p)
        es_all.close()
        return nc

    p.do("vector", lambda v, S: v.memset(SB["Sloc"][:], 0.0))
    p.do("vector", lambda v, S: v.memset(SB["carry"][:], 0.0))
    p.do("vector", lambda v, S: v.memset(SB["pacc"][:], 1.0))
    es34 = contextlib.ExitStack()
    sb(es34, "w1s", [128, KC * 160])
    sb(es34, "w2s", [128, C])
    sb(es34, "w2s4", [32, C])
    sb(es34, "tdw1s", [128, KC * 64])
    sb(es34, "tdw2s", [64, C])
    sb(es34, "th_s", [64, TT])
    sb(es34, "m_s", [128, TT]); sb(es34, "m4_s", [32, TT])
    sb(es34, "cumt", [128, KC * 257])
    sb(es34, "xn_tile", [128, KC * 257])
    sb(es34, "xinp_a", [128, KC * TT]); sb(es34, "xinp_b", [128, KC * TT])
    sb(es34, "panel_a", [128, KC * 128]); sb(es34, "panel_b", [128, KC * 128])
    sb(es34, "xxx_a", [128, TT]); sb(es34, "xxx_b", [128, TT])
    sb(es34, "mixe", [128, TT])
    sb(es34, "rate_fm", [128, TT])
    sb(es34, "r_fm", [128, KC * TT]); sb(es34, "k_fm", [128, KC * TT])
    sb(es34, "vg_fm", [128, KC * TT])
    sb(es34, "v_tm0", [128, C]); sb(es34, "v_tm1", [128, C])
    sb(es34, "g_tm", [128, 512])
    sb(es34, "y0t", [128, 512]); sb(es34, "rc_t", [128, 512])
    sb(es34, "Er", [128, CH]); sb(es34, "Ek", [128, CH]); sb(es34, "Eh", [128, CH])
    sb(es34, "rt_t", [128, CH]); sb(es34, "kt_t", [128, CH]); sb(es34, "kh_t", [128, CH])
    sb(es34, "rmid", [128, 64]); sb(es34, "ruk", [128, CH])
    sb(es34, "A2m", [128, CH]); sb(es34, "Bm", [64, CH]); sb(es34, "khTm", [128, CH])
    sb(es34, "d_m", [128, 32]); sb(es34, "dv", [128, CH])

    for kc in range(KC):
        dma("sync", "dW", lambda kc=kc: SB["w1s"][:, kc * 160:(kc + 1) * 160],
            lambda kc=kc: D["w1"][kc * 128:(kc + 1) * 128, :])
        dma("sync", "dW", lambda kc=kc: SB["tdw1s"][:, kc * 64:(kc + 1) * 64],
            lambda kc=kc: D["tdw1"][kc * 128:(kc + 1) * 128, :])
    dma("sync", "dW", lambda: SB["w2s"][:], lambda: D["w2f"][0:128, :])
    dma("sync", "dW", lambda: SB["w2s4"][:], lambda: D["w2f"][128:160, :])
    dma("sync", "dW", lambda: SB["tdw2s"][:], lambda: D["tdw2"][:])
    t_c34 = p.tok("dW")
    for e in ("tensor", "vector", "scalar"):
        p.wait(e, "dW", t_c34)

    PROJ = [("w", 0, None), ("r", 3, "wrp"), ("k", 1, "wkp"), ("v", 2, "wvp"), ("g", 4, "wgp")]

    def emit_mix_and_xinput(fam, xinp_name, vname):
        for kc in range(KC):
            koff = kc * 257
            p.wait_all("tensor", ["vector"])
            if fam < 4:
                def mixmm(t, S, kc=kc, fam=fam):
                    tp = (96, 0) if fam == 3 else None
                    return t.matmul(PS[2][:, 0:TT],
                                    SB["w2s"][fam * 32:(fam + 1) * 32, kc * 128:(kc + 1) * 128],
                                    SB["m_s"][fam * 32:(fam + 1) * 32, :],
                                    start=True, stop=True, tile_position=tp)
            else:
                def mixmm(t, S, kc=kc):
                    return t.matmul(PS[2][:, 0:TT], SB["w2s4"][0:32, kc * 128:(kc + 1) * 128],
                                    SB["m4_s"][0:32, :], start=True, stop=True)
            tT = p.dob("tensor", mixmm)
            p.wait("scalar", "p_tensor", tT)
            tA = p.dob("scalar", lambda s, S, kc=kc, vname=vname: s.activation(
                SB["mixe"][:], PS[2][:, 0:TT], AF.Identity, bias=vcol(vname, kc), scale=1.0))
            p.wait("vector", "p_scalar", tA)
            xxx = "xxx_a" if kc % 2 == 0 else "xxx_b"
            p.do("vector", lambda v, S, koff=koff, xxx=xxx: v.tensor_tensor(
                SB[xxx][:], SB["xn_tile"][:, koff:koff + TT],
                SB["xn_tile"][:, koff + 1:koff + 1 + TT], OP.subtract))
            p.do("vector", lambda v, S, xxx=xxx: v.tensor_tensor(
                SB[xxx][:], SB[xxx][:], SB["mixe"][:], OP.mult))
            p.dob("vector", lambda v, S, kc=kc, koff=koff, xxx=xxx, xinp_name=xinp_name: v.tensor_tensor(
                SB[xinp_name][:, kc * TT:(kc + 1) * TT], SB[xxx][:],
                SB["xn_tile"][:, koff + 1:koff + 1 + TT], OP.add))

    def emit_proj_mms(wname, xinp_name, dest):
        for n in range(KC):
            pan = "panel_a" if n % 2 == 0 else "panel_b"
            pnm = "pn%d" % (n % 2)
            p.wait_all("sync", ["tensor"])
            tl = dma("sync", pnm, lambda pan=pan: SB[pan][:],
                     lambda wname=wname, n=n: D[wname][n, :, :])
            p.wait("tensor", pnm, tl)
            pb = PS[4 + n % 2]
            p.wait_all("tensor", ["vector"])
            for kc in range(KC):
                inc = [("p_tensor", 1)] if kc == KC - 1 else None
                toks = p.do("tensor", lambda t, S, pan=pan, kc=kc, pb=pb, xinp_name=xinp_name: t.matmul(
                    pb[:, 0:TT], SB[pan][:, kc * 128:(kc + 1) * 128],
                    SB[xinp_name][:, kc * TT:(kc + 1) * TT],
                    start=(kc == 0), stop=(kc == KC - 1)), inc=inc)
            p.wait("vector", "p_tensor", toks["p_tensor"])
            p.dob("vector", lambda v, S, n=n, pb=pb, dest=dest: v.tensor_copy(
                SB[dest][:, n * TT:(n + 1) * TT], pb[:, 0:TT]))

    def emit_rate(xinp_name):
        p.wait_all("tensor", ["vector", "scalar"])
        for kc in range(KC):
            inc = [("p_tensor", 1)] if kc == KC - 1 else None
            toks = p.do("tensor", lambda t, S, kc=kc, xinp_name=xinp_name: t.matmul(
                PS[6][0:64, 0:TT], SB["tdw1s"][:, kc * 64:(kc + 1) * 64],
                SB[xinp_name][:, kc * TT:(kc + 1) * TT],
                start=(kc == 0), stop=(kc == KC - 1)), inc=inc)
        p.wait("scalar", "p_tensor", toks["p_tensor"])
        tA = p.dob("scalar", lambda s, S: s.activation(SB["th_s"][:], PS[6][0:64, 0:TT], AF.Tanh))
        p.wait("tensor", "p_scalar", tA)
        for n in range(KC):
            pb = PS[4 + n % 2]
            p.wait_all("tensor", ["vector", "scalar"])
            tT = p.dob("tensor", lambda t, S, n=n, pb=pb: t.matmul(
                pb[:, 0:TT], SB["tdw2s"][0:64, n * 128:(n + 1) * 128], SB["th_s"][:],
                start=True, stop=True))
            p.wait("scalar", "p_tensor", tT)
            tA = p.dob("scalar", lambda s, S, n=n, pb=pb: s.activation(
                SB["rate_fm"][:], pb[:, 0:TT], AF.Exp, bias=vcol("tdec", n), scale=1.0))
            p.wait("vector", "p_scalar", tA)
            noff = n * 257
            p.do("vector", lambda v, S, n=n, noff=noff: v.tensor_copy(
                SB["cumt"][:, noff:noff + 1], SB["carry"][:, n:n + 1]))
            p.do("vector", lambda v, S, noff=noff: v.tensor_tensor_scan(
                SB["cumt"][:, noff + 1:noff + 1 + TT], SB["rate_fm"][:], SB["rate_fm"][:],
                SB["cumt"][:, noff:noff + 1], OP.add, OP.bypass))
            p.dob("vector", lambda v, S, n=n, noff=noff: v.tensor_copy(
                SB["carry"][:, n:n + 1], SB["cumt"][:, noff + TT:noff + TT + 1]))

    def emit_transposes(src, dests, act=None, spill_gc=None):
        for n in range(KC):
            for ch in range(2):
                pb = PS[2 + (n * 2 + ch) % 2]
                p.wait_all("tensor", ["vector", "scalar"])
                tT = p.dob("tensor", lambda t, S, n=n, ch=ch, pb=pb, src=src: t.transpose(
                    pb[:, 0:128], SB[src][:, n * TT + ch * 128:n * TT + ch * 128 + 128],
                    SB["idents"][:]))
                if act is None:
                    p.wait("vector", "p_tensor", tT)
                    p.dob("vector", lambda v, S, n=n, ch=ch, pb=pb, dests=dests: v.tensor_copy(
                        SB[dests[ch]][:, n * 128:(n + 1) * 128], pb[:, 0:128]))
                else:
                    p.wait("scalar", "p_tensor", tT)
                    p.dob("scalar", lambda s, S, n=n, ch=ch, pb=pb, dests=dests, act=act: s.activation(
                        SB[dests[ch]][:, n * 128:(n + 1) * 128], pb[:, 0:128], act))

    def emit_g(tt):
        for ch in range(2):
            gc = tt * 2 + ch
            for n in range(KC):
                pb = PS[2 + n % 2]
                rs = "r%d" % (n % 4)
                p.wait_all("tensor", ["scalar", "gpsimd", "vector"])
                tT = p.dob("tensor", lambda t, S, n=n, ch=ch, pb=pb: t.transpose(
                    pb[:, 0:128], SB["vg_fm"][:, n * TT + ch * 128:n * TT + ch * 128 + 128],
                    SB["idents"][:]))
                p.wait("scalar", "p_tensor", tT)
                p.wait("scalar", rs, p.tok(rs))
                tA = p.dob("scalar", lambda s, S, n=n, pb=pb: s.activation(
                    SB["g_tm"][:, (n % 4) * 128:(n % 4) * 128 + 128], pb[:, 0:128], AF.Silu))
                p.wait("gpsimd", "p_scalar", tA)
                dma("gpsimd", rs,
                    lambda gc=gc, n=n: D["sc_g"][gc, :, n * 128:(n + 1) * 128],
                    lambda n=n: SB["g_tm"][:, (n % 4) * 128:(n % 4) * 128 + 128])

    def emit_scan_chunk(tt, ch):
        gc = tt * 2 + ch
        vt = "v_tm0" if ch == 0 else "v_tm1"
        rc_toks, y0_toks = [], []
        tT = 0
        for kc in range(KC):
            koff = kc * 257
            co = koff + 128 * ch
            p.wait_all("vector", ["scalar", "tensor"])
            for jn in range(3):
                inc = [("p_vector", 1)] if jn == 2 else None
                toksn = p.do("vector", lambda v, S, co=co, jn=jn: v.tensor_scalar(
                    SB["negs"][:, jn:jn + 1], SB["cumt"][:, co + 64 * jn:co + 64 * jn + 1],
                    -1.0, None, OP.mult), inc=inc)
            tV = toksn["p_vector"]
            p.wait("scalar", "p_vector", tV)
            p.do("scalar", lambda s, S, co=co: s.activation(
                SB["Er"][:, 0:64], SB["cumt"][:, co:co + 64], AF.Exp,
                bias=SB["cumt"][:, co:co + 1], scale=-1.0))
            p.do("scalar", lambda s, S, co=co: s.activation(
                SB["Er"][:, 64:128], SB["cumt"][:, co + 64:co + 128], AF.Exp,
                bias=SB["cumt"][:, co + 64:co + 65], scale=-1.0))
            p.do("scalar", lambda s, S, co=co: s.activation(
                SB["Ek"][:, 0:64], SB["cumt"][:, co + 1:co + 65], AF.Exp,
                bias=SB["negs"][:, 0:1], scale=1.0))
            p.do("scalar", lambda s, S, co=co: s.activation(
                SB["Ek"][:, 64:128], SB["cumt"][:, co + 65:co + 129], AF.Exp,
                bias=SB["negs"][:, 1:2], scale=1.0))
            p.do("scalar", lambda s, S, co=co: s.activation(
                SB["Eh"][:], SB["cumt"][:, co + 1:co + 129], AF.Exp,
                bias=SB["negs"][:, 2:3], scale=1.0))
            p.do("scalar", lambda s, S, co=co: s.activation(
                SB["emid"][:], SB["cumt"][:, co:co + 1], AF.Exp, bias=SB["negs"][:, 1:2], scale=1.0))
            tA = p.dob("scalar", lambda s, S, co=co: s.activation(
                SB["pch"][:], SB["cumt"][:, co:co + 1], AF.Exp, bias=SB["negs"][:, 2:3], scale=1.0))
            p.wait("vector", "p_scalar", tA)
            roff = kc * TT + 128 * ch
            p.do("vector", lambda v, S, roff=roff: v.tensor_tensor(
                SB["rt_t"][:], SB["r_fm"][:, roff:roff + 128], SB["Er"][:], OP.mult))
            p.do("vector", lambda v, S, roff=roff: v.tensor_tensor(
                SB["kt_t"][:], SB["k_fm"][:, roff:roff + 128], SB["Ek"][:], OP.mult))
            p.do("vector", lambda v, S, roff=roff: v.tensor_tensor(
                SB["kh_t"][:], SB["k_fm"][:, roff:roff + 128], SB["Eh"][:], OP.mult))
            p.do("vector", lambda v, S: v.tensor_scalar(
                SB["rmid"][:], SB["rt_t"][:, 64:128], SB["emid"][:], None, OP.mult))
            p.do("vector", lambda v, S, roff=roff, kc=kc: v.scalar_tensor_tensor(
                SB["ruk"][:], SB["r_fm"][:, roff:roff + 128], vcol("u", kc),
                SB["k_fm"][:, roff:roff + 128], OP.mult, OP.mult))
            rs2 = "r%d" % (kc % 4)
            p.wait("vector", rs2, p.tok(rs2))
            rco = (kc % 4) * 128
            p.do("vector", lambda v, S, kc=kc, rco=rco: v.tensor_scalar(
                SB["rc_t"][:, rco:rco + 64], SB["rt_t"][:, 0:64],
                SB["pacc"][:, kc:kc + 1], None, OP.mult))
            p.do("vector", lambda v, S, kc=kc, rco=rco: v.tensor_scalar(
                SB["rc_t"][:, rco + 64:rco + 128], SB["rmid"][:],
                SB["pacc"][:, kc:kc + 1], None, OP.mult))
            tV = p.dob("vector", lambda v, S, kc=kc: v.tensor_tensor(
                SB["pacc"][:, kc:kc + 1], SB["pacc"][:, kc:kc + 1], SB["pch"][:], OP.mult))
            p.wait("gpsimd", "p_vector", tV)
            dma("gpsimd", rs2,
                lambda gc=gc, kc=kc: D["sc_rc"][gc, :, kc * 128:(kc + 1) * 128],
                lambda rco=rco: SB["rc_t"][:, rco:rco + 128])
            if scan_lvl < 2:
                continue
            if "1" not in scan_parts:
                p.wait("tensor", "p_vector", tV)
            psA = PS[kc % 2]
            psB = PS[2 + kc % 2]
            if "d" in scan_parts:
                p.do("tensor", lambda t, S, kc=kc: t.matmul(
                    PS[7][:, 2 * kc:2 * kc + 2], SB["ruk"][:], SB["dcols2s"][:],
                    start=True, stop=True))
            if "A" in scan_parts or "a" in scan_parts or "z" in scan_parts:
                combos = [(0, 0, 0, 0, 0), (0, 64, 64, 64, 0), (64, 0, 0, 0, 64), (64, 64, 64, 64, 64)]
                if "a" in scan_parts:
                    combos = combos[:1]
                elif "z" in scan_parts:
                    combos = combos[3:]
                for (ih, sh, th, orow, ocol) in combos:
                    p.do("tensor", lambda t, S, ih=ih, sh=sh, th=th, orow=orow, ocol=ocol, psA=psA: t.matmul(
                        psA[orow:orow + 64, ocol:ocol + 64], SB["kt_t"][ih:ih + 64, sh:sh + 64],
                        SB["rt_t"][ih:ih + 64, th:th + 64], start=True, stop=True,
                        tile_position=(ih, orow)))
            if "B" in scan_parts:
                for ih, ocol in [(0, 0), (64, 64)]:
                    p.do("tensor", lambda t, S, ih=ih, ocol=ocol, psB=psB: t.matmul(
                        psB[0:64, ocol:ocol + 64], SB["kt_t"][ih:ih + 64, 0:64],
                        SB["rmid"][ih:ih + 64, :], start=True, stop=True))
            if "T" in scan_parts:
                p.do("tensor", lambda t, S, psB=psB: t.transpose(
                    psB[:, 128:256], SB["kh_t"][:], SB["idents"][:]))
            tT = p.softmark("tensor") if "2" not in scan_parts else tT
            if "e" in scan_parts:
                p.wait("vector", "p_tensor", tT)
                p.do("vector", lambda v, S, psA=psA: v.tensor_tensor(
                    SB["A2m"][:], psA[:, 0:128], SB["mask4s"][:], OP.mult))
                p.do("vector", lambda v, S, psB=psB: v.tensor_copy(SB["Bm"][:], psB[0:64, 0:128]))
                p.do("vector", lambda v, S, psB=psB: v.tensor_copy(SB["khTm"][:], psB[:, 128:256]))
            if "3" not in scan_parts:
                tV = p.dob("vector", lambda v, S, kc=kc: v.tensor_copy(
                    SB["d_m"][:, 2 * kc:2 * kc + 2], PS[7][:, 2 * kc:2 * kc + 2]) if "e" in scan_parts
                    else v.tensor_copy(SB["d_m"][:, 0:2], SB["dcols2s"][:]))
            if scan_lvl < 3:
                continue
            p.wait("tensor", "p_vector", tV)
            psY = PS[4 + kc % 2]
            seq = []
            for h in range(2):
                ih = h * 64
                jc = kc * 128 + h * 64
                seq += [
                    ("A2m", 0, ih, vt, 0, jc, 0, ih),
                    ("rt_t", ih, 0, "Sloc", ih, kc * 64, 0, ih),
                ]
            for h in range(2):
                ih = h * 64
                jc = kc * 128 + h * 64
                seq += [
                    ("A2m", 64, ih, vt, 64, jc, 64, ih),
                    ("Bm", 0, ih, vt, 0, jc, 64, ih),
                    ("rmid", ih, 0, "Sloc", ih, kc * 64, 64, ih),
                ]
            for q, (ln, lr, lc, rn, rr, rc2, orow, ocol) in enumerate(seq):
                p.do("tensor", lambda t, S, ln=ln, lr=lr, lc=lc, rn=rn, rr=rr, rc2=rc2,
                     orow=orow, ocol=ocol, q=q, psY=psY: t.matmul(
                    psY[orow:orow + 64, ocol:ocol + 64],
                    SB[ln][lr:lr + 64, lc:lc + 64], SB[rn][rr:rr + 64, rc2:rc2 + 64],
                    start=(q in (0, 4)), stop=(q in (3, 9)),
                    tile_position=(lr, orow)))
            psS = PS[6]
            for h in range(2):
                ih = h * 64
                jc = kc * 128 + h * 64
                inc = [("p_tensor", 1)] if h == 1 else None
                toks = p.do("tensor", lambda t, S, ih=ih, jc=jc, psS=psS, kc=kc, vt=vt: t.matmul(
                    psS[ih:ih + 64, (kc % 2) * 64:(kc % 2) * 64 + 64],
                    SB["khTm"][:, ih:ih + 64], SB[vt][:, jc:jc + 64],
                    start=True, stop=True, tile_position=(0, ih)), inc=inc)
            if scan_lvl < 4:
                continue
            p.wait("vector", "p_tensor", toks["p_tensor"])
            p.do("vector", lambda v, S, kc=kc, vt=vt: v.tensor_scalar(
                SB["dv"][:, 0:64], SB[vt][:, kc * 128:kc * 128 + 64],
                SB["d_m"][:, 2 * kc:2 * kc + 1], None, OP.mult))
            p.do("vector", lambda v, S, kc=kc, vt=vt: v.tensor_scalar(
                SB["dv"][:, 64:128], SB[vt][:, kc * 128 + 64:kc * 128 + 128],
                SB["d_m"][:, 2 * kc + 1:2 * kc + 2], None, OP.mult))
            p.do("vector", lambda v, S, rco=rco, psY=psY: v.tensor_tensor(
                SB["y0t"][:, rco:rco + 128], psY[:, 0:128], SB["dv"][:], OP.add))
            tS = p.dob("vector", lambda v, S, kc=kc, psS=psS: v.scalar_tensor_tensor(
                SB["Sloc"][:, kc * 64:(kc + 1) * 64], SB["Sloc"][:, kc * 64:(kc + 1) * 64],
                SB["pch"][:], psS[:, (kc % 2) * 64:(kc % 2) * 64 + 64], OP.mult, OP.add))
            p.wait("gpsimd", "p_vector", tS)
            dma("gpsimd", rs2,
                lambda gc=gc, kc=kc: D["sc_y0"][gc, :, kc * 128:(kc + 1) * 128],
                lambda rco=rco: SB["y0t"][:, rco:rco + 128])

    for tt in range(NTT if p3_ntt is None else p3_ntt):
        t0 = tt * TT
        p.wait_all("sync", ["vector", "scalar", "tensor"])
        for kc in range(KC):
            dma("sync", "dX", lambda kc=kc: SB["xn_tile"][:, kc * 257:(kc + 1) * 257],
                lambda kc=kc, t0=t0: D["sc_xn"][kc, :, t0:t0 + 257])
        t_xn = p.tok("dX")
        p.wait("vector", "dX", t_xn)
        for kc in range(KC):
            koff = kc * 257
            xxx = "xxx_a" if kc % 2 == 0 else "xxx_b"
            p.wait_all("vector", ["tensor"])
            p.do("vector", lambda v, S, koff=koff, xxx=xxx: v.tensor_tensor(
                SB[xxx][:], SB["xn_tile"][:, koff:koff + TT],
                SB["xn_tile"][:, koff + 1:koff + 1 + TT], OP.subtract))
            tV = p.dob("vector", lambda v, S, kc=kc, xxx=xxx, koff=koff: v.scalar_tensor_tensor(
                SB[xxx][:], SB[xxx][:], vcol("x", kc),
                SB["xn_tile"][:, koff + 1:koff + 1 + TT], OP.mult, OP.add))
            p.wait("tensor", "p_vector", tV)
            p.do("tensor", lambda t, S, kc=kc, xxx=xxx: t.matmul(
                PS[0][:, 0:TT], SB["w1s"][:, kc * 160:kc * 160 + 128], SB[xxx][:],
                start=(kc == 0), stop=(kc == KC - 1)))
            inc = [("p_tensor", 1)] if kc == KC - 1 else None
            toks = p.do("tensor", lambda t, S, kc=kc, xxx=xxx: t.matmul(
                PS[1][0:32, 0:TT], SB["w1s"][:, kc * 160 + 128:kc * 160 + 160], SB[xxx][:],
                start=(kc == 0), stop=(kc == KC - 1)), inc=inc)
        p.wait("scalar", "p_tensor", toks["p_tensor"])
        p.do("scalar", lambda s, S: s.activation(SB["m_s"][:], PS[0][:, 0:TT], AF.Tanh))
        tA = p.dob("scalar", lambda s, S: s.activation(SB["m4_s"][:], PS[1][0:32, 0:TT], AF.Tanh))
        p.wait("tensor", "p_scalar", tA)
        p.wait("vector", "p_scalar", tA)

        if not p3_skip_proj:
            xi = ["xinp_a", "xinp_b"]
            for pi, (pn, fam, wname) in enumerate(PROJ):
                xinp = xi[pi % 2]
                emit_mix_and_xinput(fam, xinp, pn)
                if pn == "w":
                    emit_rate(xinp)
                elif pn == "r":
                    emit_proj_mms(wname, xinp, "r_fm")
                elif pn == "k":
                    emit_proj_mms(wname, xinp, "k_fm")
                elif pn == "v":
                    emit_proj_mms(wname, xinp, "vg_fm")
                    emit_transposes("vg_fm", ("v_tm0", "v_tm1"))
                else:
                    emit_proj_mms(wname, xinp, "vg_fm")
                    emit_g(tt)

        if not (p3_skip_scan or p3_skip_proj):
            for ch in range(2):
                emit_scan_chunk(tt, ch)
        p.barrier()

    for s in ("r0", "r1", "r2", "r3"):
        p.wait("gpsimd", s, p.tok(s))
    p.wait("gpsimd", "dG", p.tok("dG"))
    p.barrier()

    if stop_after == "P3":
        dma("gpsimd", "dG", lambda: D["sc_y0"][0, :, :], lambda: SB["r_fm"][:, 0:2048])
        dma("gpsimd", "dG", lambda: D["sc_y0"][1, :, :], lambda: SB["r_fm"][:, 2048:4096])
        dma("gpsimd", "dG", lambda: D["sc_y0"][2, :, :], lambda: SB["k_fm"][:, 0:2048])
        dma("gpsimd", "dG", lambda: D["sc_y0"][3, :, :], lambda: SB["k_fm"][:, 2048:4096])
        dma("gpsimd", "dG", lambda: D["sc_y0"][4, :, :], lambda: SB["v_tm0"][:])
        dma("gpsimd", "dG", lambda: D["sc_y0"][5, :, :], lambda: SB["v_tm1"][:])
        dma("gpsimd", "dG", lambda: D["sc_y0"][6, :, :], lambda: SB["cumt"][:, 0:2048])
        dma("gpsimd", "dG", lambda: D["sc_y0"][7, :, :], lambda: SB["cumt"][:, 2048:4096])
        dma("gpsimd", "dG", lambda: D["sc_rc"][0, :, 0:1024], lambda: SB["Sloc"][:])
        dma("gpsimd", "dG", lambda: D["sc_rc"][1, :, 0:16], lambda: SB["pacc"][:])
        p.barrier()
        es34.close()
        _emit_taps(nc, p, D, SB, taps, dma)
        _finalize(nc, p)
        es_all.close()
        return nc

    es34.close()

    sb(es_all, "Sin", [128, KC * 64])
    p.do("vector", lambda v, S: v.memset(SB["Sin"][:], 0.0))
    p.wait_all("gpsimd", ["vector"])
    for kc in range(KC):
        dma("gpsimd", "dG", lambda kc=kc: D["cc_in"][kc * 128:(kc + 1) * 128, :],
            lambda kc=kc: SB["Sloc"][:, kc * 64:(kc + 1) * 64])
    dma("gpsimd", "dG", lambda: D["cc_in"][2048:2176, 0:16], lambda: SB["pacc"][:])
    dma("gpsimd", "dG", lambda: D["cc_in"][2048:2176, 16:64], lambda: SB["Sin"][:, 0:48])
    p.wait("gpsimd", "dG", p.tok("dG"))
    p.do("gpsimd", lambda g, S: g.collective_compute(
        "AllGather", OP.bypass, replica_groups=[list(range(NCORES))],
        ins=[D["cc_in"][:]], outs=[D["cc_out"][:]]), inc=[("CC", 1)])
    p.wait("sync", "CC", 1)

    with contextlib.ExitStack() as es5:
        sb(es5, "ejt", [128, 64])
        sb(es5, "ejm", [128, 64]); sb(es5, "pj_all", [128, KC]); sb(es5, "pjtm", [128, KC])
        for j in range(NCORES):
            p.wait_all("sync", ["vector"])
            tp = dma("sync", "dS", lambda: SB["pj_all"][:],
                     lambda j=j: D["cc_out"][j * 2176 + 2048:j * 2176 + 2176, 0:16])
            p.wait("vector", "dS", tp)
            p.do("vector", lambda v, S, j=j: v.tensor_scalar(
                SB["pjtm"][:], SB["pj_all"][:], SB["lts"][:, j:j + 1],
                SB["omlts"][:, j:j + 1], OP.mult, OP.add))
            for kc in range(KC):
                p.wait_all("sync", ["vector"])
                tl = dma("sync", "dS", lambda: SB["ejt"][:],
                         lambda j=j, kc=kc: D["cc_out"][j * 2176 + kc * 128:j * 2176 + (kc + 1) * 128, :])
                p.wait("vector", "dS", tl)
                p.do("vector", lambda v, S, j=j: v.tensor_scalar(
                    SB["ejm"][:], SB["ejt"][:], SB["lts"][:, j:j + 1], None, OP.mult))
                p.dob("vector", lambda v, S, kc=kc: v.scalar_tensor_tensor(
                    SB["Sin"][:, kc * 64:(kc + 1) * 64], SB["Sin"][:, kc * 64:(kc + 1) * 64],
                    SB["pjtm"][:, kc:kc + 1], SB["ejm"][:], OP.mult, OP.add))
    p.barrier()

    if stop_after == "P5":
        dma("gpsimd", "dG", lambda: D["sc_y0"][0, :, 0:1024], lambda: SB["Sin"][:])
        p.barrier()
        _emit_taps(nc, p, D, SB, taps, dma)
        _finalize(nc, p)
        es_all.close()
        return nc

    es6 = contextlib.ExitStack()
    sb(es6, "y0r0", [128, C]); sb(es6, "y0r1", [128, C])
    sb(es6, "gr0", [128, C]); sb(es6, "gr1", [128, C])
    sb(es6, "rcr0", [128, C]); sb(es6, "rcr1", [128, C])
    sb(es6, "lnwbs", [128, C]); sb(es6, "lnbbs", [128, C])
    sb(es6, "g1t", [128, C]); sb(es6, "g2t", [128, C])
    sb(es6, "qt", [128, C]); sb(es6, "qfm", [128, C])
    sb(es6, "wopan_a", [128, C]); sb(es6, "wopan_b", [128, C])
    sb(es6, "s_t", [128, C]); sb(es6, "trash6", [128, 64])
    sb(es6, "gn_st", [128, 128])

    dma("sync", "dS", lambda: SB["lnwbs"][:], lambda: D["lnw_bc"][:])
    dma("sync", "dS", lambda: SB["lnbbs"][:], lambda: D["lnb_bc"][:])

    for tt in range(NTT):
        for ch in range(2):
            gc = tt * 2 + ch
            y0r = "y0r0" if ch == 0 else "y0r1"
            gr = "gr0" if ch == 0 else "gr1"
            rcr = "rcr0" if ch == 0 else "rcr1"
            p.wait_all("sync", ["vector", "tensor"])
            dma("sync", "dX", lambda y0r=y0r: SB[y0r][:], lambda gc=gc: D["sc_y0"][gc, :, :])
            dma("sync", "dX", lambda gr=gr: SB[gr][:], lambda gc=gc: D["sc_g"][gc, :, :])
            t3 = dma("sync", "dX", lambda rcr=rcr: SB[rcr][:], lambda gc=gc: D["sc_rc"][gc, :, :])
            p.wait("tensor", "dX", t3)
            for kc in range(KC):
                pb = PS[kc % 2]
                p.wait_all("tensor", ["vector"])
                for q, (ih, toff, orow) in enumerate([(0, 0, 0), (64, 0, 0), (0, 64, 64), (64, 64, 64)]):
                    inc = [("p_tensor", 1)] if q == 3 else None
                    toks = p.do("tensor", lambda t, S, kc=kc, pb=pb, ih=ih, toff=toff, orow=orow,
                                rcr=rcr, q=q: t.matmul(
                        pb[orow:orow + 64, ih:ih + 64],
                        SB[rcr][ih:ih + 64, kc * 128 + toff:kc * 128 + toff + 64],
                        SB["Sin"][ih:ih + 64, kc * 64:(kc + 1) * 64],
                        start=(q in (0, 2)), stop=(q in (1, 3)), tile_position=(ih, orow)), inc=inc)
                p.wait("vector", "p_tensor", toks["p_tensor"])
                p.wait("vector", "dX", t3)
                p.dob("vector", lambda v, S, kc=kc, pb=pb, y0r=y0r: v.tensor_tensor(
                    SB[y0r][:, kc * 128:(kc + 1) * 128], SB[y0r][:, kc * 128:(kc + 1) * 128],
                    pb[:, 0:128], OP.add))
            for h in range(32):
                p.do("vector", lambda v, S, h=h, y0r=y0r: v.reduce_sum(
                    SB["gn_st"][:, h:h + 1], SB[y0r][:, h * 64:(h + 1) * 64], axis=AX.X))
                p.do("vector", lambda v, S, h=h, y0r=y0r: v.tensor_tensor(
                    SB["trash6"][:], SB[y0r][:, h * 64:(h + 1) * 64],
                    SB[y0r][:, h * 64:(h + 1) * 64], OP.mult))
                p.do("vector", lambda v, S, h=h: v.reduce_sum(
                    SB["gn_st"][:, 32 + h:33 + h], SB["trash6"][:], axis=AX.X))
            p.do("vector", lambda v, S: v.tensor_scalar(
                SB["gn_st"][:, 64:96], SB["gn_st"][:, 0:32], 1.0 / 64, None, OP.mult))
            p.do("vector", lambda v, S: v.tensor_tensor(
                SB["gn_st"][:, 96:128], SB["gn_st"][:, 64:96], SB["gn_st"][:, 64:96], OP.mult))
            p.do("vector", lambda v, S: v.tensor_scalar(
                SB["gn_st"][:, 0:32], SB["gn_st"][:, 32:64], 1.0 / 64, None, OP.mult))
            tV = p.dob("vector", lambda v, S: v.tensor_tensor(
                SB["gn_st"][:, 0:32], SB["gn_st"][:, 0:32], SB["gn_st"][:, 96:128], OP.subtract))
            p.wait("scalar", "p_vector", tV)
            tA = p.dob("scalar", lambda s, S: s.activation(
                SB["gn_st"][:, 32:64], SB["gn_st"][:, 0:32], AF.Sqrt, bias=SB["epsss"][:, 1:2], scale=1.0))
            p.wait("vector", "p_scalar", tA)
            p.do("vector", lambda v, S: v.reciprocal(SB["gn_st"][:, 0:32], SB["gn_st"][:, 32:64]))
            p.do("vector", lambda v, S, gr=gr: v.tensor_tensor(SB["g1t"][:], SB[gr][:], SB["lnwbs"][:], OP.mult))
            p.do("vector", lambda v, S, gr=gr: v.tensor_tensor(SB["g2t"][:], SB[gr][:], SB["lnbbs"][:], OP.mult))
            for h in range(32):
                p.do("vector", lambda v, S, h=h, y0r=y0r: v.tensor_scalar(
                    SB["qt"][:, h * 64:(h + 1) * 64], SB[y0r][:, h * 64:(h + 1) * 64],
                    SB["gn_st"][:, 64 + h:65 + h], SB["gn_st"][:, h:h + 1], OP.subtract, OP.mult))
            p.do("vector", lambda v, S: v.tensor_tensor(SB["qt"][:], SB["qt"][:], SB["g1t"][:], OP.mult))
            tV = p.dob("vector", lambda v, S: v.tensor_tensor(SB["qt"][:], SB["qt"][:], SB["g2t"][:], OP.add))
            p.wait("tensor", "p_vector", tV)
            for kc in range(KC):
                pb = PS[2 + kc % 2]
                p.wait_all("tensor", ["vector"])
                tT = p.dob("tensor", lambda t, S, kc=kc, pb=pb: t.transpose(
                    pb[:, 0:128], SB["qt"][:, kc * 128:(kc + 1) * 128], SB["idents"][:]))
                p.wait("vector", "p_tensor", tT)
                p.dob("vector", lambda v, S, kc=kc, pb=pb: v.tensor_copy(
                    SB["qfm"][:, kc * 128:(kc + 1) * 128], pb[:, 0:128]))
            for kc in range(KC):
                pan = "wopan_a" if kc % 2 == 0 else "wopan_b"
                pnm = "pn%d" % (kc % 2)
                p.wait_all("sync", ["tensor"])
                tl = dma("sync", pnm, lambda pan=pan: SB[pan][:],
                         lambda kc=kc: D["wo"][kc * 128:(kc + 1) * 128, :])
                p.wait("tensor", pnm, tl)
                if kc == 0:
                    p.wait_all("tensor", ["vector"])
                for n in range(4):
                    inc = [("p_tensor", 1)] if (kc == KC - 1 and n == 3) else None
                    toks = p.do("tensor", lambda t, S, kc=kc, n=n, pan=pan: t.matmul(
                        PS[4 + n][:, 0:512], SB["qfm"][:, kc * 128:(kc + 1) * 128],
                        SB[pan][:, n * 512:(n + 1) * 512],
                        start=(kc == 0), stop=(kc == KC - 1)), inc=inc)
            p.wait("vector", "p_tensor", toks["p_tensor"])
            for n in range(4):
                p.dob("vector", lambda v, S, n=n: v.tensor_copy(
                    SB["s_t"][:, n * 512:(n + 1) * 512], PS[4 + n][:, 0:512]))
            p.wait_all("gpsimd", ["vector"])
            dma("gpsimd", "dG", lambda gc=gc: D["sc_s"][gc, :, :], lambda: SB["s_t"][:])
            p.wait("vector", "dG", p.tok("dG"))
    p.barrier()
    es6.close()

    if stop_after == "P6":
        _emit_taps(nc, p, D, SB, taps, dma)
        _finalize(nc, p)
        es_all.close()
        return nc

    with contextlib.ExitStack() as esq:
        sb(esq, "fpan", [128, C]); sb(esq, "fpan2", [128, C])
        sb(esq, "gacc", [128, 144])
        sb(esq, "thr", [128, 3]); sb(esq, "nthr", [128, 3]); sb(esq, "gs3", [128, 3])
        sb(esq, "qb_t", [128, C]); sb(esq, "qpan", [128, C], F16)
        sb(esq, "trash7", [128, C])

        pieces = []
        for r in range(44):
            pieces.append((0, "fw1", r, "sc_qw1", None))
        for r in range(44):
            pieces.append((1, "fw2", r, "sc_qw2", None))
        for r in range(16):
            for jj in range(3):
                pieces.append((2, "fw3", r, "sc_qw3", jj))

        def piece_w(pc):
            return 2048 if (pc[4] is None or pc[4] < 2) else MF - 4096

        def piece_src(pc):
            mi, dn, r, scn, jj = pc
            if jj is None:
                return lambda: D[dn][r * 128:(r + 1) * 128, :]
            w = piece_w(pc)
            return lambda: D[dn][r * 128:(r + 1) * 128, jj * 2048:jj * 2048 + w]

        for ip, pc in enumerate(pieces):
            pan = "fpan" if ip % 2 == 0 else "fpan2"
            pnm = "pn%d" % (ip % 2)
            w = piece_w(pc)
            p.wait_all("sync", ["scalar", "vector"])
            tl = dma("sync", pnm, lambda pan=pan, w=w: SB[pan][:, 0:w], piece_src(pc))
            p.wait("scalar", pnm, tl)
            tA = p.dob("scalar", lambda s, S, pan=pan, w=w: s.activation(
                SB["trash7"][:, 0:w], SB[pan][:, 0:w], AF.Abs))
            p.wait("vector", "p_scalar", tA)
            p.dob("vector", lambda v, S, ip=ip, w=w: v.reduce_sum(
                SB["gacc"][:, ip:ip + 1], SB["trash7"][:, 0:w], axis=AX.X))
        tA = p.tok("p_vector")
        p.wait("vector", "p_vector", 0)
        p.do("vector", lambda v, S: v.reduce_sum(SB["gs3"][:, 0:1], SB["gacc"][:, 0:44], axis=AX.X))
        p.do("vector", lambda v, S: v.reduce_sum(SB["gs3"][:, 1:2], SB["gacc"][:, 44:88], axis=AX.X))
        tV = p.dob("vector", lambda v, S: v.reduce_sum(SB["gs3"][:, 2:3], SB["gacc"][:, 88:136], axis=AX.X))
        p.wait("tensor", "p_vector", tV)
        tT = p.dob("tensor", lambda t, S: t.matmul(
            PS[0][:, 0:3], SB["ones128s"][:], SB["gs3"][:], start=True, stop=True))
        p.wait("vector", "p_tensor", tT)
        p.do("vector", lambda v, S: v.tensor_scalar(
            SB["thr"][:], PS[0][:, 0:3], 0.5 / (MF * C), 0.5 * FFN_EPS, OP.mult, OP.add))
        tV = p.dob("vector", lambda v, S: v.tensor_scalar(SB["nthr"][:], SB["thr"][:], -1.0, None, OP.mult))
        for ip, pc in enumerate(pieces):
            mi, dn, r, scn, jj = pc
            pan = "fpan" if ip % 2 == 0 else "fpan2"
            pnm = "pn%d" % (ip % 2)
            w = piece_w(pc)
            p.wait_all("sync", ["vector"])
            tl = dma("sync", pnm, lambda pan=pan, w=w: SB[pan][:, 0:w], piece_src(pc))
            p.wait("vector", pnm, tl)
            p.do("vector", lambda v, S, pan=pan, mi=mi, w=w: v.tensor_scalar(
                SB["qb_t"][:, 0:w], SB[pan][:, 0:w], SB["nthr"][:, mi:mi + 1], None, OP.is_lt))
            tq = p.dob("vector", lambda v, S, pan=pan, mi=mi, w=w: v.scalar_tensor_tensor(
                SB["qpan"][:, 0:w], SB[pan][:, 0:w], SB["thr"][:, mi:mi + 1],
                SB["qb_t"][:, 0:w], OP.is_gt, OP.subtract))
            p.wait("gpsimd", "p_vector", tq)
            if jj is None:
                tg = dma("gpsimd", "dG", lambda scn=scn, r=r: D[scn][r * 128:(r + 1) * 128, :],
                         lambda: SB["qpan"][:, 0:2048])
            else:
                tg = dma("gpsimd", "dG",
                         lambda scn=scn, r=r, jj=jj, w=w: D[scn][r * 128:(r + 1) * 128, jj * 2048:jj * 2048 + w],
                         lambda w=w: SB["qpan"][:, 0:w])
            p.wait("vector", "dG", tg)
    p.barrier()

    es7 = contextlib.ExitStack()
    sb(es7, "f_sb", [128, 2 * C])
    sb(es7, "xn7", [128, FT])
    sb(es7, "xnh", [128, KC * FT], F16); sb(es7, "xnl", [128, KC * FT], F16)
    sb(es7, "dkt", [128, FT]); sb(es7, "dku", [128, FT]); sb(es7, "dkh32", [128, FT])
    sb(es7, "q1g", [128, KC * 1024], F16)
    sb(es7, "q2g", [128, KC * 1024], F16)
    sb(es7, "q3g", [128, 8 * C], F16)
    sb(es7, "sig_t", [128, FT]); sb(es7, "hid32", [128, FT])
    sb(es7, "hidh", [128, 8 * FT], F16); sb(es7, "hidl", [128, 8 * FT], F16)
    sb(es7, "gf_t", [128, C]); sb(es7, "xre", [128, C]); sb(es7, "sre", [128, C]); sb(es7, "o_t", [128, C])

    MGRPS = [(0, 1024), (1024, 1024), (2048, 1024), (3072, 1024), (4096, 1024), (5120, 512)]

    for ft in range(NFT):
        t0 = ft * FT
        for kc in range(KC):
            p.wait_all("sync", ["vector"])
            tl = dma("sync", "dX", lambda: SB["xn7"][:],
                     lambda kc=kc, t0=t0: D["sc_xn"][kc, :, t0 + 1:t0 + 1 + FT])
            p.wait("vector", "dX", tl)
            p.do("vector", lambda v, S: v.tensor_scalar(SB["dkt"][:], SB["xn7"][:], 4097.0, None, OP.mult))
            p.do("vector", lambda v, S: v.tensor_tensor(SB["dku"][:], SB["dkt"][:], SB["xn7"][:], OP.subtract))
            p.do("vector", lambda v, S: v.tensor_tensor(SB["dkh32"][:], SB["dkt"][:], SB["dku"][:], OP.subtract))
            p.do("vector", lambda v, S, kc=kc: v.tensor_copy(SB["xnh"][:, kc * FT:(kc + 1) * FT], SB["dkh32"][:]))
            p.dob("vector", lambda v, S, kc=kc: v.tensor_tensor(
                SB["xnl"][:, kc * FT:(kc + 1) * FT], SB["xn7"][:], SB["dkh32"][:], OP.subtract))
        p.wait_all("vector", ["scalar"])
        tV = p.dob("vector", lambda v, S: v.memset(SB["f_sb"][:], 0.0))
        p.wait("tensor", "p_vector", tV)

        for (m0, mw) in MGRPS:
            nmc = mw // 128
            p.wait_all("sync", ["tensor"])
            for kc in range(KC):
                dma("sync", "dS", lambda kc=kc, mw=mw: SB["q1g"][:, kc * 1024:kc * 1024 + mw],
                    lambda kc=kc, m0=m0, mw=mw: D["sc_qw1"][m0:m0 + mw, kc * 128:(kc + 1) * 128],
                    transpose=True)
                dma("sync", "dS", lambda kc=kc, mw=mw: SB["q2g"][:, kc * 1024:kc * 1024 + mw],
                    lambda kc=kc, m0=m0, mw=mw: D["sc_qw2"][m0:m0 + mw, kc * 128:(kc + 1) * 128],
                    transpose=True)
            t_q12 = p.tok("dS")
            p.wait("tensor", "dS", t_q12)
            for mc in range(nmc):
                ps1 = PS[(mc % 2) * 2]
                ps2 = PS[(mc % 2) * 2 + 1]
                p.wait_all("tensor", ["vector", "scalar"])
                for kc in range(KC):
                    for hl, xb in ((0, "xnh"), (1, "xnl")):
                        p.do("tensor", lambda t, S, kc=kc, mc=mc, xb=xb, ps1=ps1: t.matmul(
                            ps1[:, 0:FT], SB["q1g"][:, kc * 1024 + mc * 128:kc * 1024 + (mc + 1) * 128],
                            SB[xb][:, kc * FT:(kc + 1) * FT],
                            start=(kc == 0 and xb == "xnh"), stop=(kc == KC - 1 and xb == "xnl")))
                for kc in range(KC):
                    for hl, xb in ((0, "xnh"), (1, "xnl")):
                        inc = [("p_tensor", 1)] if (kc == KC - 1 and hl == 1) else None
                        toks = p.do("tensor", lambda t, S, kc=kc, mc=mc, xb=xb, ps2=ps2: t.matmul(
                            ps2[:, 0:FT], SB["q2g"][:, kc * 1024 + mc * 128:kc * 1024 + (mc + 1) * 128],
                            SB[xb][:, kc * FT:(kc + 1) * FT],
                            start=(kc == 0 and xb == "xnh"), stop=(kc == KC - 1 and xb == "xnl")), inc=inc)
                p.wait("scalar", "p_tensor", toks["p_tensor"])
                tA = p.dob("scalar", lambda s, S, ps1=ps1: s.activation(SB["sig_t"][:], ps1[:, 0:FT], AF.Sigmoid))
                p.wait("vector", "p_scalar", tA)
                p.do("vector", lambda v, S, ps2=ps2: v.tensor_tensor(SB["hid32"][:], SB["sig_t"][:], ps2[:, 0:FT], OP.mult))
                p.do("vector", lambda v, S: v.tensor_scalar(SB["dkt"][:], SB["hid32"][:], 4097.0, None, OP.mult))
                p.do("vector", lambda v, S: v.tensor_tensor(SB["dku"][:], SB["dkt"][:], SB["hid32"][:], OP.subtract))
                p.do("vector", lambda v, S: v.tensor_tensor(SB["dkh32"][:], SB["dkt"][:], SB["dku"][:], OP.subtract))
                p.do("vector", lambda v, S, mc=mc: v.tensor_copy(SB["hidh"][:, mc * FT:(mc + 1) * FT], SB["dkh32"][:]))
                p.dob("vector", lambda v, S, mc=mc: v.tensor_tensor(
                    SB["hidl"][:, mc * FT:(mc + 1) * FT], SB["hid32"][:], SB["dkh32"][:], OP.subtract))
            p.wait_all("sync", ["tensor"])
            for mc in range(nmc):
                dma("sync", "dS", lambda mc=mc: SB["q3g"][:, mc * C:(mc + 1) * C],
                    lambda mc=mc, m0=m0: D["sc_qw3"][:, m0 + mc * 128:m0 + (mc + 1) * 128],
                    transpose=True)
            t_q3 = p.tok("dS")
            p.wait("tensor", "dS", t_q3)
            p.wait_all("tensor", ["vector"])
            for tsub in range(2):
                for n in range(4):
                    pb = PS[4 + (2 * tsub + n) % 4]
                    p.wait_all("tensor", ["vector"])
                    for mc in range(nmc):
                        for hl, hb in ((0, "hidh"), (1, "hidl")):
                            inc = [("p_tensor", 1)] if (mc == nmc - 1 and hl == 1) else None
                            toks = p.do("tensor", lambda t, S, tsub=tsub, n=n, mc=mc, hb=hb, pb=pb: t.matmul(
                                pb[:, 0:512], SB[hb][:, mc * FT + tsub * 128:mc * FT + tsub * 128 + 128],
                                SB["q3g"][:, mc * C + n * 512:mc * C + (n + 1) * 512],
                                start=(mc == 0 and hb == "hidh"), stop=(mc == nmc - 1 and hb == "hidl")), inc=inc)
                    p.wait("vector", "p_tensor", toks["p_tensor"])
                    p.dob("vector", lambda v, S, tsub=tsub, n=n, pb=pb: v.tensor_tensor(
                        SB["f_sb"][:, tsub * C + n * 512:tsub * C + (n + 1) * 512],
                        SB["f_sb"][:, tsub * C + n * 512:tsub * C + (n + 1) * 512],
                        pb[:, 0:512], OP.add))
        for tsub in range(2):
            gc = ft * 2 + tsub
            p.wait_all("scalar", ["vector"])
            tA = p.dob("scalar", lambda s, S, tsub=tsub: s.activation(
                SB["gf_t"][:], SB["f_sb"][:, tsub * C:(tsub + 1) * C], AF.Silu))
            p.wait_all("sync", ["vector"])
            t1 = dma("sync", "dS", lambda: SB["xre"][:],
                     lambda t0=t0, tsub=tsub: D["xin"][1 + t0 + tsub * 128:1 + t0 + (tsub + 1) * 128, :])
            t2 = dma("sync", "dS", lambda: SB["sre"][:], lambda gc=gc: D["sc_s"][gc, :, :])
            p.wait("vector", "p_scalar", tA)
            p.wait("vector", "dS", t2)
            p.do("vector", lambda v, S: v.tensor_tensor(SB["o_t"][:], SB["sre"][:], SB["gf_t"][:], OP.mult))
            tV = p.dob("vector", lambda v, S: v.tensor_tensor(SB["o_t"][:], SB["o_t"][:], SB["xre"][:], OP.add))
            p.wait("gpsimd", "p_vector", tV)
            dma("gpsimd", "dG",
                lambda t0=t0, tsub=tsub: D["outp"][t0 + tsub * 128:t0 + (tsub + 1) * 128, :],
                lambda: SB["o_t"][:])
    es7.close()

    _emit_taps(nc, p, D, SB, taps, dma)
    p.wait("gpsimd", "dG", p.tok("dG"))
    p.barrier()
    _finalize(nc, p)
    es_all.close()
    return nc


def _emit_taps(nc, p, D, SB, taps, dma):
    if not taps:
        return
    p.barrier()
    with contextlib.ExitStack() as est:
        bounce = est.enter_context(nc.sbuf_tensor("tapb", [128, 4096], F32))
        for name in taps:
            t = D[name]
            dims = list(t.shape)
            R = int(np.prod(dims[:-1]))
            Cc = dims[-1]
            out = nc.dram_tensor("tap_" + name, [R, Cc], t.dtype, kind="ExternalOutput").ap()
            flat = t.rearrange("a b c -> (a b) c") if len(dims) == 3 else t
            for i in range((R + 127) // 128):
                r0, r1 = i * 128, min((i + 1) * 128, R)
                tl = dma("sync", "dS", lambda r0=r0, r1=r1, Cc=Cc, bounce=bounce: bounce[0:r1 - r0, 0:Cc],
                         lambda flat=flat, r0=r0, r1=r1: flat[r0:r1, :])
                p.wait("gpsimd", "dS", tl)
                tg = dma("gpsimd", "dG", lambda out=out, r0=r0, r1=r1: out[r0:r1, :],
                         lambda r0=r0, r1=r1, Cc=Cc, bounce=bounce: bounce[0:r1 - r0, 0:Cc])
                p.wait("sync", "dG", tg)
        p.wait("gpsimd", "dG", p.tok("dG"))


def _finalize(nc, p):
    with contextlib.ExitStack() as es2:
        S = {}
        for nm in p.semnames:
            S[nm] = es2.enter_context(nc.semaphore(nm))
        block = es2.enter_context(nc.Block())

        def mk(eng_name):
            def body(e):
                for fn in p.q[eng_name]:
                    fn(e, S)
            return body
        block.sync(mk("sync"))
        block.tensor(mk("tensor"))
        block.vector(mk("vector"))
        block.scalar(mk("scalar"))
        block.gpsimd(mk("gpsimd"))
    return nc


def kernel(**inputs):
    in_maps = host_inputs(inputs)
    nc = build_program(taps=_BUILD_OPTS["taps"], stop_after=_BUILD_OPTS["stop_after"],
                       p3_skip_proj=_BUILD_OPTS["p3_skip_proj"],
                       p3_skip_scan=_BUILD_OPTS["p3_skip_scan"],
                       p3_ntt=_BUILD_OPTS["p3_ntt"], scan_lvl=_BUILD_OPTS["scan_lvl"])
    res = run_bass_kernel_spmd(nc, in_maps, core_ids=list(range(NCORES)))
    out = np.concatenate([np.asarray(r["outp"]) for r in res.results], axis=0)
    if _BUILD_OPTS["taps"] or _BUILD_OPTS["stop_after"]:
        kernel._tap_results = res.results
    return out.reshape(1, NCORES * TC, C)
